# version 31
# baseline (speedup 1.0000x reference)
import os
import numpy as np
import ml_dtypes
from contextlib import ExitStack

_DBG = bool(os.environ.get("KDBG"))

import concourse.bass as bass
import concourse.mybir as mybir
from concourse.bass_utils import run_bass_kernel_spmd

F32 = mybir.dt.float32
BF16 = mybir.dt.bfloat16
AF = mybir.ActivationFunctionType
ALU = mybir.AluOpType
AX = mybir.AxisListType

ENGS = ("sync", "tensor", "vector", "scalar", "gpsimd")

S = 8192
OWN = 4096
C = 1024
CCN = 8
G = 512
NG = S // G
QC = 256
NQ = OWN // QC
NKT = S // 128
ND = OWN // G
EPS = 1e-5


class Sem:
    def __init__(self, nc, ctx, name):
        self.h = ctx.enter_context(nc.semaphore(name))
        self.count = 0


class Sched:

    def __init__(self):
        self.items = {e: [] for e in ENGS}

    def add(self, eng, fn=None, waits=(), incs=()):
        for s, n in incs:
            s.count += n
        self.items[eng].append((fn, tuple(waits), tuple(incs)))

    def emit(self, nc, name):
        with nc.Block(name) as block:
            for eng in ENGS:
                items = self.items[eng]

                def body(E, items=items):
                    for fn, waits, incs in items:
                        for s, v in waits:
                            E.wait_ge(s.h, v)
                        inst = fn(E) if fn is not None else None
                        for s, n in incs:
                            if inst is None:
                                E.sem_inc(s.h, n)
                            else:
                                inst = inst.then_inc(s.h, n)

                getattr(block, eng)(body)


def add_barrier(sc, sm):
    sc.add("sync", None, waits=((sm["ld"], sm["ld"].count),))
    sc.add("gpsimd", None, waits=((sm["st"], sm["st"].count),))
    sc.add("tensor", None, waits=((sm["pe"], sm["pe"].count),))
    sc.add("vector", None, waits=((sm["dve"], sm["dve"].count),))
    sc.add("scalar", None, waits=((sm["act"], sm["act"].count),))
    for e in ENGS:
        sc.add(e, None, incs=((sm["ph"], 1),))
    tgt = sm["ph"].count
    for e in ENGS:
        sc.add(e, None, waits=((sm["ph"], tgt),))


def build_program():
    nc = bass.Bass("TRN2", target_bir_lowering=False)

    x_d = nc.dram_tensor("x", [S, C], F32, kind="ExternalInput").ap()
    xh_d = nc.dram_tensor("xh", [1, C], F32, kind="ExternalInput").ap()
    wqkv_d = nc.dram_tensor("wqkv", [3 * C, C], F32, kind="ExternalInput").ap()
    wout_d = nc.dram_tensor("wout", [C, C], F32, kind="ExternalInput").ap()
    wr_d = nc.dram_tensor("wr", [C, C], F32, kind="ExternalInput").ap()
    wkf_d = nc.dram_tensor("wkf", [C, C], F32, kind="ExternalInput").ap()
    wvf_d = nc.dram_tensor("wvf", [C, C], F32, kind="ExternalInput").ap()
    mus_d = nc.dram_tensor("mus", [1, 2 * C], F32, kind="ExternalInput").ap()
    identb_d = nc.dram_tensor("identb", [128, 128], BF16, kind="ExternalInput").ap()
    identf_d = nc.dram_tensor("identf", [128, 128], F32, kind="ExternalInput").ap()
    onesb_d = nc.dram_tensor("onesb", [128, 1], BF16, kind="ExternalInput").ap()
    ones1b_d = nc.dram_tensor("ones1b", [1, 1], BF16, kind="ExternalInput").ap()
    onesrf_d = nc.dram_tensor("onesrf", [1, 128], F32, kind="ExternalInput").ap()
    onescf_d = nc.dram_tensor("onescf", [128, 1], F32, kind="ExternalInput").ap()
    ones1f_d = nc.dram_tensor("ones1f", [1, 1], F32, kind="ExternalInput").ap()

    y_d = nc.dram_tensor("y", [OWN, C], F32, kind="ExternalOutput").ap()

    _skind = "ExternalOutput" if _DBG else "Internal"
    wscr = nc.dram_tensor("wscr", [7, CCN, 128, C], BF16, kind=_skind).ap()
    xnt_scr = nc.dram_tensor("xnt_scr", [CCN, 128, S], BF16, kind=_skind).ap()
    v_scr = nc.dram_tensor("v_scr", [S, C], BF16, kind=_skind).ap()
    ctx_scr = nc.dram_tensor("ctx_scr", [CCN, 128, OWN], BF16, kind=_skind).ap()
    if _DBG:
        dbg_dinv = nc.dram_tensor("dbg_dinv", [128, 32], F32, kind="ExternalOutput").ap()
        dbg_negc = nc.dram_tensor("dbg_negc", [128, 1], F32, kind="ExternalOutput").ap()
        dbg_cs = nc.dram_tensor("dbg_cs", [1, 8], F32, kind="ExternalOutput").ap()
        dbg_ck2 = nc.dram_tensor("dbg_ck2", [1, 1], F32, kind="ExternalOutput").ap()
        dbg_xnhT = nc.dram_tensor("dbg_xnhT", [128, CCN], BF16, kind="ExternalOutput").ap()
        dbg_negs = nc.dram_tensor("dbg_negs", [1, 64], F32, kind="ExternalOutput").ap()
        dbg_rstd = nc.dram_tensor("dbg_rstd", [128, 64], F32, kind="ExternalOutput").ap()
        dbg_mean = nc.dram_tensor("dbg_mean", [128, 64], F32, kind="ExternalOutput").ap()
        dbg_ssq = nc.dram_tensor("dbg_ssq", [128, 64], F32, kind="ExternalOutput").ap()
        dbg_tmp4 = nc.dram_tensor("dbg_tmp4", [128, 64], F32, kind="ExternalOutput").ap()

    with ExitStack() as octx:
        sm = {n: Sem(nc, octx, n) for n in ("ld", "st", "pe", "act", "dve", "ph")}
        ld, st, pe, act, dve = sm["ld"], sm["st"], sm["pe"], sm["act"], sm["dve"]

        osb = lambda *a: octx.enter_context(nc.sbuf_tensor(*a))
        identb = osb("identb_s", [128, 128], BF16)
        identf = osb("identf_s", [128, 128], F32)
        onesb = osb("onesb_s", [128, 1], BF16)
        ones1b = osb("ones1b_s", [1, 1], BF16)
        onesrf = osb("onesrf_s", [1, 128], F32)
        onescf = osb("onescf_s", [128, 1], F32)
        ones1f = osb("ones1f_s", [1, 1], F32)
        mur_sb = osb("mur_sb", [128, CCN], F32)
        muk_sb = osb("muk_sb", [128, CCN], F32)
        ck2 = osb("ck2", [1, 1], F32)
        xnhT = osb("xnhT", [128, CCN], BF16)
        dinv_all = osb("dinv_all", [128, 32], F32)
        negc = osb("negc", [128, 1], F32)
        den_sb = osb("den_sb", [1, 256], F32)
        negs_sb = osb("negs_sb", [1, 64], F32) if _DBG else None
        cs = osb("cs", [1, 8], F32)
        dmy = osb("dmy", [128, 8], F32)
        dmyr = osb("dmyr", [1, 256], F32)
        hstat = osb("hstat", [1, 8], F32)
        meanall = osb("meanall", [128, 64], F32)
        rstdall = osb("rstdall", [128, 64], F32)

        with ExitStack() as ctx:
            sb = lambda *a: ctx.enter_context(nc.sbuf_tensor(*a))
            ps = lambda *a: ctx.enter_context(nc.psum_tensor(*a))
            wtmp = [sb(f"wtmp{i}", [128, CCN, C], F32) for i in range(3)]
            wstage = [sb(f"wstage{i}", [128, CCN, 128], BF16) for i in range(2)]
            murow = sb("murow", [1, 2 * C], F32)
            hsq = sb("hsq", [1, C], F32)
            xh_p = sb("xh_p", [1, C], F32)
            xnh_row = sb("xnh_row", [1, C], BF16)
            ps_w = [ps(f"ps_w{i}", [128, 512], F32) for i in range(4)]
            ps_mu = ps("ps_mu", [128, 512], F32)

            sc = Sched()
            for dst, src in ((identb, identb_d), (identf, identf_d), (onesb, onesb_d),
                             (ones1b, ones1b_d), (onesrf, onesrf_d), (ones1f, ones1f_d),
                             (murow, mus_d), (onescf, onescf_d)):
                sc.add("sync", lambda E, d=dst, s=src: E.dma_start(out=d[:], in_=s),
                       incs=((ld, 16),))
            consts_ld = ld.count

            def mu_mms(E):
                inst = None
                for m in range(2):
                    for cc in range(CCN):
                        i = m * CCN + cc
                        inst = E.matmul(ps_mu[:, i:i + 1],
                                        murow[0:1, m * C + cc * 128: m * C + (cc + 1) * 128],
                                        ones1f[:], start=True, stop=True)
                return inst
            sc.add("tensor", mu_mms, waits=((ld, consts_ld),), incs=((pe, 1),))
            mu_pe = pe.count
            sc.add("scalar", lambda E: E.copy(mur_sb[:], ps_mu[:, 0:CCN]),
                   waits=((pe, mu_pe),))
            sc.add("scalar", lambda E: E.copy(muk_sb[:], ps_mu[:, CCN:2 * CCN]),
                   incs=((act, 1),))
            mucp_v = act.count

            wqkv3 = wqkv_d.rearrange("(h three) c -> three h c", three=3)
            srcs = [wqkv3[0], wqkv3[1], wqkv3[2], wout_d, wr_d, wkf_d, wvf_d]

            cast_v = {}; tp_v = {}; stg_v = {}; sto_v = {}
            wld_v = {}
            for m in range(7):
                w = [(pe, tp_v[(m - 3) * CCN + CCN - 1])] if m >= 3 else []

                def wld(E, m=m):
                    E.dma_start(out=wtmp[m % 3][:, 0:4, :],
                                in_=srcs[m][0:512, :].rearrange("(i p) c -> p i c", p=128)
                                ).then_inc(ld.h, 16)
                    return E.dma_start(out=wtmp[m % 3][:, 4:8, :],
                                       in_=srcs[m][512:1024, :].rearrange("(i p) c -> p i c", p=128))
                sc.add("sync", wld, waits=tuple(w), incs=((ld, 16),))
                ld.count += 16
                wld_v[m] = ld.count

                for dblk in range(CCN):
                    it = m * CCN + dblk
                    r = it & 1

                    def tmm(E, m=m, dblk=dblk, r=r):
                        inst = None
                        for cc in range(CCN):
                            inst = E.matmul(ps_w[r * 2 + cc // 4][:, (cc % 4) * 128:(cc % 4 + 1) * 128],
                                            wtmp[m % 3][:, dblk, cc * 128:(cc + 1) * 128],
                                            identf[:], start=True, stop=True)
                        return inst
                    w = [(ld, wld_v[m] - (16 if dblk < 4 else 0))] + \
                        ([(act, stg_v[it - 2])] if it >= 2 else [])
                    sc.add("tensor", tmm, waits=tuple(w), incs=((pe, 1),))
                    tp_v[it] = pe.count
                    cast_v[it] = 0

                    def stg(E, r=r):
                        E.copy(wstage[r][:, 0:4, :], ps_w[r * 2][:].rearrange("p (i c) -> p i c", i=4))
                        return E.copy(wstage[r][:, 4:8, :], ps_w[r * 2 + 1][:].rearrange("p (i c) -> p i c", i=4))
                    w = [(pe, tp_v[it])] + ([(st, sto_v[it - 2])] if it >= 2 else [])
                    sc.add("scalar", stg, waits=tuple(w), incs=((act, 1),))
                    stg_v[it] = act.count

                    def sto(E, m=m, dblk=dblk, r=r):
                        E.dma_start(
                            out=wscr[m, 0:4, :, dblk * 128:(dblk + 1) * 128].rearrange("cc p d -> p cc d"),
                            in_=wstage[r][:, 0:4, :]).then_inc(st.h, 16)
                        return E.dma_start(
                            out=wscr[m, 4:8, :, dblk * 128:(dblk + 1) * 128].rearrange("cc p d -> p cc d"),
                            in_=wstage[r][:, 4:8, :])
                    sc.add("gpsimd", sto, waits=((act, stg_v[it]),), incs=((st, 16),))
                    st.count += 16
                    sto_v[it] = st.count

            sc.add("sync", lambda E: E.dma_start(out=xh_p[:], in_=xh_d), incs=((ld, 16),))
            xh_ld = ld.count
            sc.add("vector", lambda E: E.reduce_sum(hstat[0:1, 0:1], xh_p[:], axis=AX.X),
                   waits=((ld, xh_ld),))
            sc.add("vector", lambda E: E.tensor_mul(hsq[:], xh_p[:], xh_p[:]),
                   waits=((ld, xh_ld),), incs=((dve, 1),))
            sc.add("vector", lambda E: E.reduce_sum(hstat[0:1, 1:2], hsq[:], axis=AX.X),
                   waits=((dve, dve.count),), incs=((dve, 1),))
            sc.add("vector", lambda E: E.tensor_scalar(hstat[0:1, 2:3], hstat[0:1, 0:1], 1.0 / C, None, ALU.mult),
                   waits=((dve, dve.count),), incs=((dve, 1),))
            sc.add("vector", lambda E: E.tensor_scalar(hstat[0:1, 3:4], hstat[0:1, 1:2], 1.0 / C, EPS, ALU.mult, ALU.add),
                   incs=((dve, 1),))
            sc.add("vector", lambda E: E.tensor_mul(hstat[0:1, 4:5], hstat[0:1, 2:3], hstat[0:1, 2:3]),
                   waits=((dve, dve.count),), incs=((dve, 1),))
            sc.add("vector", lambda E: E.tensor_sub(hstat[0:1, 5:6], hstat[0:1, 3:4], hstat[0:1, 4:5]),
                   waits=((dve, dve.count),), incs=((dve, 1),))
            hv = dve.count
            sc.add("scalar", lambda E: E.activation(hstat[0:1, 6:7], hstat[0:1, 5:6], AF.Ln),
                   waits=((dve, hv),), incs=((act, 1),))
            sc.add("scalar", lambda E: E.activation(hstat[0:1, 7:8], hstat[0:1, 6:7], AF.Exp, scale=-0.5),
                   waits=((act, act.count),), incs=((act, 1),))
            hv2 = act.count
            sc.add("vector", lambda E: E.tensor_scalar(
                xnh_row[:], xh_p[:], hstat[0:1, 2:3], hstat[0:1, 7:8],
                ALU.subtract, ALU.mult),
                waits=((act, hv2),), incs=((dve, 1),))
            hv4 = dve.count

            def hT(E):
                for cc in range(CCN):
                    E.matmul(ps_mu[:, 16 + cc:17 + cc],
                             xnh_row[0:1, cc * 128:(cc + 1) * 128],
                             ones1b[:], start=True, stop=True)
                E.matmul(ps_mu[:, 32:160], onesrf[:], onesrf[:], start=True, stop=True)
                return E.matmul(ps_mu[:, 32:160], onesrf[:], onesrf[:], start=True, stop=True)
            sc.add("tensor", hT, waits=((dve, hv4), (act, mucp_v)), incs=((pe, 1),))
            hv5 = pe.count
            sc.add("scalar", lambda E: E.copy(xnhT[:], ps_mu[:, 16:16 + CCN]),
                   waits=((pe, hv5),), incs=((act, 1),))

            add_barrier(sc, sm)
            sc.emit(nc, "p0")

        mctx = ExitStack()
        ktres = mctx.enter_context(nc.sbuf_tensor("ktres", [128, CCN, S], BF16))

        with ExitStack() as ctx:
            sb = lambda *a: ctx.enter_context(nc.sbuf_tensor(*a))
            ps = lambda *a: ctx.enter_context(nc.psum_tensor(*a))
            wkT = sb("wkT", [128, CCN, C], BF16)
            wvT = sb("wvT", [128, CCN, C], BF16)
            xin = [sb(f"xin{i}", [128, 2, C], F32) for i in range(2)]
            xnb = sb("xnb", [128, 4, C], BF16)
            xnT = sb("xnT", [128, CCN, G], BF16)
            knsq = [sb(f"knsq{i}", [128, G], BF16) for i in range(2)]
            vout = [sb(f"vout{i}", [128, 2, C], BF16) for i in range(2)]
            bnraw = sb("bnraw", [128, 2, 2, 6], F32)
            bnmv = sb("bnmv", [128, 2, 2], F32)
            tmp2 = sb("tmp2", [128, 2], F32)
            lnv = sb("lnv", [128, 2], F32)
            ps_t = [ps(f"ps_t{i}", [128, 512], F32) for i in range(4)]
            ps_kt = [ps(f"ps_kt{i}", [128, 512], F32) for i in range(2)]
            ps_kn = ps("ps_kn", [128, 512], F32)
            ps_v = ps("ps_v", [128, 512], F32)

            sc = Sched()

            ld_hg = {}
            apply_v = {}
            tp_v = {}
            evac_v = {}
            ktmm_v = {}
            ktev_v = {}
            knsq_v = {}
            knmm_v = {}
            knrd_v = {}
            xnst_v = {}
            vmm_v = {}
            vev_v = {}
            vst_v = {}

            def emit_xload(g):
                for half in range(2):
                    hg = 2 * g + half
                    w = []
                    if hg >= 2:
                        gp, hp = (hg - 2) // 2, (hg - 2) % 2
                        w.append((dve, apply_v[(gp, hp * 2 + 1)]))
                    sc.add("sync", lambda E, hg=hg: E.dma_start(
                        out=xin[hg & 1][:],
                        in_=x_d[hg * 256:(hg + 1) * 256, :].rearrange("(i p) c -> p i c", p=128)),
                        waits=tuple(w), incs=((ld, 16),))
                    ld_hg[hg] = ld.count

            ln_v = {}
            rstd_v = {}

            def emit_stats_chunk(hg, j):
                sc.add("vector", lambda E, hg=hg, j=j: E.bn_stats(
                    bnraw[:, j // 2, j % 2, :],
                    xin[hg & 1][:, j // 2, (j % 2) * 512:(j % 2 + 1) * 512]),
                    waits=((ld, ld_hg[hg]),), incs=((dve, 1),))

            def emit_stats_tail(hg):
                sc.add("vector", lambda E: [E.bn_aggr(bnmv[:, 0, :], bnraw[:, 0, :, :]),
                                            E.bn_aggr(bnmv[:, 1, :], bnraw[:, 1, :, :])][-1],
                       waits=((dve, dve.count),), incs=((dve, 1),))
                co = 2 * hg
                w = [(dve, dve.count)]
                if hg >= 1:
                    w.append((act, ln_v[hg - 1]))
                sc.add("vector", lambda E: E.tensor_scalar(
                    tmp2[:], bnmv[:, :, 1], 1.0, EPS, ALU.mult, ALU.add),
                    waits=tuple(w), incs=((dve, 1),))
                tmp2_v = dve.count
                sc.add("vector", lambda E, co=co: E.tensor_copy(
                    meanall[:, co:co + 2], bnmv[:, :, 0]),
                    incs=((dve, 1),))
                sc.add("scalar", lambda E: E.activation(lnv[:], tmp2[:], AF.Ln),
                       waits=((dve, tmp2_v),), incs=((act, 1),))
                ln_v[hg] = act.count
                sc.add("scalar", lambda E, co=co: E.activation(
                    rstdall[:, co:co + 2], lnv[:], AF.Exp, scale=-0.5),
                    waits=((act, act.count),), incs=((act, 1),))
                rstd_v[hg] = act.count

            def emit_stats(hg):
                for j in range(4):
                    emit_stats_chunk(hg, j)
                emit_stats_tail(hg)

            def emit_applies(g):
                for t in range(4):
                    hg = 2 * g + t // 2
                    ti = t % 2
                    co = 4 * g + t
                    w = [(ld, ld_hg[hg]), (act, rstd_v[hg])]
                    if g >= 1:
                        w.append((pe, tp_v[(g - 1) * 8 + t * 2 + 1]))
                    sc.add("vector", lambda E, hg=hg, ti=ti, t=t, co=co: E.tensor_scalar(
                        xnb[:, t, :], xin[hg & 1][:, ti, :],
                        meanall[:, co:co + 1], rstdall[:, co:co + 1], ALU.subtract, ALU.mult),
                        waits=tuple(w), incs=((dve, 1),))
                    apply_v[(g, t)] = dve.count

            emit_xload(0)
            for dst, m in ((wkT, 1), (wvT, 2)):
                for hh in range(2):
                    sc.add("sync", lambda E, d=dst, m=m, hh=hh: E.dma_start(
                        out=d[:, hh * 4:(hh + 1) * 4, :],
                        in_=wscr[m, hh * 4:(hh + 1) * 4, :, :].rearrange("cc p d -> p cc d")),
                        incs=((ld, 16),))
            wk_ld = ld.count
            emit_stats(0)
            emit_stats(1)
            emit_applies(0)
            for g in range(NG):
                for t in range(4):
                    for hh in range(2):
                        idx = g * 8 + t * 2 + hh
                        r = idx & 3

                        def tmm(E, t=t, hh=hh, r=r):
                            inst = None
                            for i in range(4):
                                cc = hh * 4 + i
                                inst = E.matmul(ps_t[r][:, i * 128:(i + 1) * 128],
                                                xnb[:, t, cc * 128:(cc + 1) * 128],
                                                identb[:], start=True, stop=True)
                            return inst
                        w = [(dve, apply_v[(g, t)])]
                        if idx >= 4 and idx % 8 >= 2:
                            w.append((dve, evac_v[idx - 4]))
                        if idx % 8 < 2 and g >= 1:
                            w.append((act, vev_v[(g - 1) * 4 + 3]))
                        sc.add("tensor", tmm, waits=tuple(w), incs=((pe, 1),))
                        tp_v[idx] = pe.count

                        w = [(pe, tp_v[idx])]
                        if idx % 8 == 0 and g >= 1:
                            w.append((st, xnst_v[g - 1]))
                        sc.add("vector", lambda E, t=t, hh=hh, r=r: E.tensor_copy(
                            xnT[:, hh * 4:(hh + 1) * 4, t * 128:(t + 1) * 128],
                            ps_t[r][:].rearrange("p (i c) -> p i c", i=4)),
                            waits=tuple(w), incs=((dve, 1),))
                        evac_v[idx] = dve.count

                def xnst(E, g=g):
                    E.dma_start(out=xnt_scr[0:4, :, g * G:(g + 1) * G].rearrange("cc p t -> p cc t"),
                                in_=xnT[:, 0:4, :]).then_inc(st.h, 16)
                    return E.dma_start(out=xnt_scr[4:8, :, g * G:(g + 1) * G].rearrange("cc p t -> p cc t"),
                                       in_=xnT[:, 4:8, :])
                sc.add("gpsimd", xnst, waits=((dve, evac_v[g * 8 + 7]),), incs=((st, 16),))
                st.count += 16
                xnst_v[g] = st.count

                if g + 1 < NG:
                    emit_xload(g + 1)

                def emit_kn_dc(gk, dc):
                    w = [(act, ktev_v[(gk, dc)])]
                    if gk * 8 + dc >= 2:
                        pidx = gk * 8 + dc - 2
                        w.append((pe, knmm_v[(pidx // 8, pidx % 8)]))
                    sc.add("vector", lambda E, gk=gk, dc=dc: E.tensor_mul(
                        knsq[dc & 1][:], ktres[:, dc, gk * G:(gk + 1) * G],
                        ktres[:, dc, gk * G:(gk + 1) * G]),
                        waits=tuple(w), incs=((dve, 1),))
                    knsq_v[(gk, dc)] = dve.count
                    w = [(dve, knsq_v[(gk, dc)])]
                    if dc == 0 and gk >= 1:
                        w.append((dve, knrd_v[gk - 1]))
                    sc.add("tensor", lambda E, gk=gk, dc=dc: E.matmul(
                        ps_kn[0:1, :], onesb[:], knsq[dc & 1][:],
                        start=(dc == 0), stop=(dc == CCN - 1)),
                        waits=tuple(w), incs=((pe, 1),))
                    knmm_v[(gk, dc)] = pe.count

                def emit_kn_tail(gk):
                    def kn_slack(E):
                        E.matmul(ps_kn[32:33, 0:128], onescf[:], identf[:], start=True, stop=True)
                        return E.matmul(ps_kn[32:33, 0:128], onescf[:], identf[:], start=True, stop=True)
                    sc.add("tensor", kn_slack, incs=((pe, 1),))
                    kn_slack_v = pe.count
                    sc.add("vector", lambda E: E.reduce_max(hstat[0:1, 0:1], ps_kn[0:1, :], axis=AX.X),
                           waits=((pe, kn_slack_v),), incs=((dve, 1),))

                    def knmax(E, gk=gk):
                        if gk == 0:
                            return E.tensor_copy(ck2[:], hstat[0:1, 0:1])
                        return E.tensor_max(ck2[:], ck2[:], hstat[0:1, 0:1])
                    sc.add("vector", knmax, waits=((dve, dve.count),), incs=((dve, 1),))
                    knrd_v[gk] = dve.count

                for dc in range(CCN):
                    w = []
                    if dc == 0:
                        w.append((dve, evac_v[g * 8 + 7]))
                        if g == 0:
                            w.append((ld, wk_ld))
                    if dc >= 2:
                        w.append((act, ktev_v[(g, dc - 2)]))

                    def ktmm(E, g=g, dc=dc):
                        inst = None
                        for cc in range(CCN):
                            inst = E.matmul(ps_kt[dc & 1][:],
                                            wkT[:, cc, dc * 128:(dc + 1) * 128],
                                            xnT[:, cc, :],
                                            start=(cc == 0), stop=(cc == CCN - 1))
                        return inst
                    sc.add("tensor", ktmm, waits=tuple(w), incs=((pe, 1),))
                    ktmm_v[(g, dc)] = pe.count
                    sc.add("scalar", lambda E, g=g, dc=dc: E.copy(
                        ktres[:, dc, g * G:(g + 1) * G], ps_kt[dc & 1][:]),
                        waits=((pe, ktmm_v[(g, dc)]),), incs=((act, 1),))
                    ktev_v[(g, dc)] = act.count
                    if g >= 1:
                        if dc < 4:
                            emit_kn_dc(g - 1, 2 * dc)
                            emit_kn_dc(g - 1, 2 * dc + 1)
                        elif dc == 4:
                            emit_kn_tail(g - 1)
                    if g + 1 < NG:
                        emit_stats_chunk(2 * g + 2 + dc // 4, dc % 4)
                        if dc == 3:
                            emit_stats_tail(2 * g + 2)
                        elif dc == 7:
                            emit_stats_tail(2 * g + 3)

                if g + 1 < NG:
                    emit_applies(g + 1)

                for tt in range(4):
                    gti = g * 4 + tt
                    pair = tt & 1
                    w = []
                    if tt == 0:
                        if g >= 1:
                            w.append((dve, knrd_v[g - 1]))
                        if g == 0:
                            w.append((ld, wk_ld))
                    if gti >= 2:
                        w.append((act, vev_v[gti - 2]))

                    def vmm(E, g=g, tt=tt, pair=pair):
                        pv = [ps_v, ps_kn] if pair == 0 else ps_t
                        inst = None
                        for cc in range(CCN):
                            for nh in range(2):
                                inst = E.matmul(pv[nh][:],
                                                xnT[:, cc, tt * 128:(tt + 1) * 128],
                                                wvT[:, cc, nh * 512:(nh + 1) * 512],
                                                start=(cc == 0), stop=(cc == CCN - 1))
                        return inst
                    sc.add("tensor", vmm, waits=tuple(w), incs=((pe, 1),))
                    vmm_v[gti] = pe.count

                    w = [(pe, vmm_v[gti])]
                    hh = tt // 2
                    if 2 * g + hh >= 2:
                        w.append((st, vst_v[2 * g + hh - 2]))

                    def vev(E, g=g, tt=tt, pair=pair, hh=hh):
                        pv = [ps_v, ps_kn] if pair == 0 else ps_t
                        E.copy(vout[(2 * g + hh) & 1][:, tt & 1, 0:512], pv[0][:])
                        return E.copy(vout[(2 * g + hh) & 1][:, tt & 1, 512:1024], pv[1][:])
                    sc.add("scalar", vev, waits=tuple(w), incs=((act, 1),))
                    vev_v[gti] = act.count

                    if tt & 1:
                        sc.add("gpsimd", lambda E, g=g, hh=hh: E.dma_start(
                            out=v_scr[g * G + hh * 256:g * G + (hh + 1) * 256, :].rearrange(
                                "(i p) d -> p i d", p=128),
                            in_=vout[(2 * g + hh) & 1][:]),
                            waits=((act, vev_v[gti]),), incs=((st, 16),))
                        vst_v[2 * g + hh] = st.count


            for dc in range(CCN):
                emit_kn_dc(NG - 1, dc)
            emit_kn_tail(NG - 1)

            add_barrier(sc, sm)
            sc.emit(nc, "a1")

        with ExitStack() as ctx:
            sb = lambda *a: ctx.enter_context(nc.sbuf_tensor(*a))
            ps = lambda *a: ctx.enter_context(nc.psum_tensor(*a))
            wqT = sb("wqT", [128, CCN, C], BF16)
            xnt_c = [sb(f"xnt_c{i}", [128, CCN, QC], BF16) for i in range(2)]
            qt = [sb(f"qt{i}", [128, CCN, QC], BF16) for i in range(2)]
            vstream = sb("vstream", [128, 8, C], BF16)
            pT = sb("pT", [128, 2, QC], BF16)
            ctxout = [sb(f"ctxout{i}", [128, CCN, QC], BF16) for i in range(2)]
            qsq = [sb(f"qsq{i}", [128, QC], BF16) for i in range(2)]
            dacc = sb("dacc", [128, QC], F32)
            ps_ctx = [ps(f"ps_ctx{i}", [128, 512], F32) for i in range(4)]
            ps_st = [ps(f"ps_st{i}", [128, 512], F32) for i in range(2)]
            ps_misc = ps("ps_misc", [128, 512], F32)
            ps_den = ps("ps_den", [128, 512], F32)

            sc = Sched()
            for hh in range(2):
                sc.add("sync", lambda E, hh=hh: E.dma_start(
                    out=wqT[:, hh * 4:(hh + 1) * 4, :],
                    in_=wscr[0, hh * 4:(hh + 1) * 4, :, :].rearrange("cc p d -> p cc d")),
                    incs=((ld, 16),))
            wq_ld = ld.count
            xntc_v = {}
            dadd_v = {}; denmm_v = {}
            qpmm_v = {}; qev_v = {}; qsq_v = {}; qnmm_v = {}
            negneg_v = {}; negcp_v = {}
            stmm_v = {}; exp_v = {}; pvmm_v = {}
            vld_v = {}
            ctxev_v = {}; denev_v = {}; dcol_v = {}; dinv_v = {}; ctxst_v = {}

            for j in range(NQ):
                w = []
                if j >= 2:
                    w.append((pe, qpmm_v[(j - 2, CCN - 1)]))

                def xcl(E, j=j):
                    E.dma_start(out=xnt_c[j & 1][:, 0:4, :],
                                in_=xnt_scr[0:4, :, j * QC:(j + 1) * QC].rearrange("cc p t -> p cc t")
                                ).then_inc(ld.h, 16)
                    return E.dma_start(out=xnt_c[j & 1][:, 4:8, :],
                                       in_=xnt_scr[4:8, :, j * QC:(j + 1) * QC].rearrange("cc p t -> p cc t"))
                sc.add("sync", xcl, waits=tuple(w), incs=((ld, 16),))
                ld.count += 16
                xntc_v[j] = ld.count

                if j >= 1:
                    emit_tail_act(j - 1)

                for dc in range(CCN):
                    w = []
                    if dc == 0:
                        w.append((ld, xntc_v[j]))
                        if j == 0:
                            w.append((ld, wq_ld))
                        if j >= 1:
                            w.append((act, exp_v[(j - 1, NKT - 1)]))
                    if dc >= 2:
                        w.append((act, qev_v[(j, dc - 2)]))

                    def qpm(E, j=j, dc=dc):
                        inst = None
                        for cc in range(CCN):
                            inst = E.matmul(ps_st[dc & 1][:, 0:QC],
                                            wqT[:, cc, dc * 128:(dc + 1) * 128],
                                            xnt_c[j & 1][:, cc, :],
                                            start=(cc == 0), stop=(cc == CCN - 1))
                        return inst
                    sc.add("tensor", qpm, waits=tuple(w), incs=((pe, 1),))
                    qpmm_v[(j, dc)] = pe.count

                    w = [(pe, qpmm_v[(j, dc)])]
                    if j >= 2 and dc == 0:
                        w.append((pe, stmm_v[(j - 2, NKT - 1)]))
                    sc.add("scalar", lambda E, j=j, dc=dc: E.activation(
                        qt[j & 1][:, dc, :], ps_st[dc & 1][:, 0:QC], AF.Copy, scale=1.0 / 32.0),
                        waits=tuple(w), incs=((act, 1),))
                    qev_v[(j, dc)] = act.count
                    if dc == 1 and j >= 1:
                        emit_denev(j - 1, qpmm_v[(j, 1)])

                for dc in range(CCN):
                    w = [(act, qev_v[(j, dc)])]
                    if j * 8 + dc >= 2:
                        pi = j * 8 + dc - 2
                        w.append((pe, qnmm_v[(pi // 8, pi % 8)]))
                    sc.add("vector", lambda E, j=j, dc=dc: E.tensor_mul(
                        qsq[dc & 1][:], qt[j & 1][:, dc, :], qt[j & 1][:, dc, :]),
                        waits=tuple(w), incs=((dve, 1),))
                    qsq_v[(j, dc)] = dve.count
                    w = [(dve, qsq_v[(j, dc)])]
                    if dc == 0 and j >= 1:
                        w.append((dve, negneg_v[j - 1]))
                    sc.add("tensor", lambda E, j=j, dc=dc: E.matmul(
                        ps_misc[0:1, QC:2 * QC], onesb[:], qsq[dc & 1][:],
                        start=(dc == 0), stop=(dc == CCN - 1)),
                        waits=tuple(w), incs=((pe, 1),))
                    qnmm_v[(j, dc)] = pe.count

                def emit_vload(kt):
                    q4 = kt // 4
                    w = []
                    if q4 >= 2:
                        w.append((pe, pvmm_v[(j, 4 * q4 - 5)]))
                    elif j >= 1:
                        w.append((pe, pvmm_v[(j - 1, 59 if q4 == 0 else 63)]))
                    slot = (q4 * 4) % 8
                    sc.add("sync", lambda E, j=j, kt=kt, slot=slot: E.dma_start(
                        out=vstream[:, slot:slot + 4, :],
                        in_=v_scr[kt * 128:kt * 128 + 512, :].rearrange("(i p) d -> p i d", p=128)),
                        waits=tuple(w), incs=((ld, 16),))
                    vld_v[(j, q4)] = ld.count

                def emit_st(kt):
                    w = []
                    if kt >= 2:
                        w.append((act, exp_v[(j, kt - 2)]))
                    else:
                        w.append((act, qev_v[(j, CCN - 1)]))
                    sc.add("tensor", lambda E, j=j, kt=kt: [
                        E.matmul(ps_st[kt & 1][:, 0:QC],
                                 ktres[:, dc, kt * 128:(kt + 1) * 128],
                                 qt[j & 1][:, dc, :],
                                 start=(dc == 0), stop=(dc == CCN - 1))
                        for dc in range(CCN)][-1],
                        waits=tuple(w), incs=((pe, 1),))
                    stmm_v[(j, kt)] = pe.count

                def emit_exp(kt):
                    w = [(pe, stmm_v[(j, kt)])]
                    if kt == 0:
                        w.append((act, negcp_v[j]))
                    if kt >= 2:
                        w.append((pe, pvmm_v[(j, kt - 2)]))
                        w.append((dve, dadd_v[(j, kt - 2)]))
                    elif j >= 1:
                        w.append((dve, dadd_v[(j - 1, NKT - 2 + kt)]))
                    sc.add("scalar", lambda E, j=j, kt=kt: E.activation(
                        pT[:, kt & 1, :], ps_st[kt & 1][:, 0:QC], AF.Exp, bias=negc[:]),
                        waits=tuple(w), incs=((act, 1),))
                    exp_v[(j, kt)] = act.count

                def emit_pv(kt):
                    w = [(act, exp_v[(j, kt)]), (ld, vld_v[(j, kt // 4)])]
                    if kt == 0:
                        if j >= 2:
                            w.append((st, ctxst_v[j - 2]))
                        if j >= 1:
                            w.append((dve, ctxev_v[j - 1]))
                            w.append((dve, dinv_v[j - 1]))

                    def pvm(E, j=j, kt=kt):
                        inst = None
                        for dc in range(CCN):
                            inst = E.matmul(ps_ctx[dc >> 1][:, (dc & 1) * QC:((dc & 1) + 1) * QC],
                                            vstream[:, kt % 8, dc * 128:(dc + 1) * 128],
                                            pT[:, kt & 1, :],
                                            start=(kt == 0 and (dc & 1) == 0),
                                            stop=(kt == NKT - 1))
                        return inst
                    sc.add("tensor", pvm, waits=tuple(w), incs=((pe, 1),))
                    pvmm_v[(j, kt)] = pe.count
                    if kt == 0:
                        wd = [(act, exp_v[(j, 0)])]
                        if j >= 1:
                            wd.append((pe, denmm_v[j - 1]))
                        sc.add("vector", lambda E, j=j: E.tensor_copy(dacc[:], pT[:, 0, :]),
                               waits=tuple(wd), incs=((dve, 1),))
                    else:
                        sc.add("vector", lambda E, j=j, kt=kt: E.tensor_add(
                            dacc[:], dacc[:], pT[:, kt & 1, :]),
                            waits=((act, exp_v[(j, kt)]),), incs=((dve, 1),))
                    dadd_v[(j, kt)] = dve.count

                def emit_tail_act(jj):
                    w = [(dve, dadd_v[(jj, NKT - 1)])]
                    if jj >= 1:
                        w.append((act, denev_v[jj - 1]))
                    sc.add("tensor", lambda E, j=jj: E.matmul(
                        ps_den[0:1, 0:QC], onescf[:], dacc[:], start=True, stop=True),
                        waits=tuple(w), incs=((pe, 1),))
                    denmm_v[jj] = pe.count

                    w = [(pe, pvmm_v[(jj, NKT - 1)]), (pe, denmm_v[jj])]
                    if j >= 2:
                        w.append((st, ctxst_v[j - 2]))

                    def ctxevd(E, j=jj):
                        inst = None
                        for c4 in range(4):
                            inst = E.tensor_copy(ctxout[j & 1][:, 2 * c4:2 * c4 + 2, :],
                                                 ps_ctx[c4][:].rearrange("p (i c) -> p i c", i=2))
                        return inst
                    sc.add("vector", ctxevd, waits=tuple(w), incs=((dve, 1),))
                    ctxev_v[jj] = dve.count

                def emit_denev(jj, slack_pe):
                    def denev(E):
                        E.copy(den_sb[:], ps_den[0:1, 0:QC])
                        E.copy(dmyr[:], den_sb[:])
                        E.copy(dmy[0:1, 4:5], dmyr[0:1, 0:1])
                        return E.copy(dmyr[:], den_sb[:])
                    sc.add("scalar", denev, waits=((pe, slack_pe),), incs=((act, 1),))
                    denev_v[jj] = act.count

                def emit_tail_pe(jj):
                    def dcol(E):
                        E.matmul(ps_den[:, QC:QC + 1], den_sb[0:1, 0:128], ones1f[:],
                                 start=True, stop=True)
                        E.matmul(ps_den[:, QC + 1:QC + 2], den_sb[0:1, 128:256], ones1f[:],
                                 start=True, stop=True)
                        E.matmul(ps_den[:, QC + 10:QC + 138], onesrf[:], onesrf[:],
                                 start=True, stop=True)
                        return E.matmul(ps_den[:, QC + 10:QC + 138], onesrf[:], onesrf[:],
                                            start=True, stop=True)
                    sc.add("tensor", dcol, waits=((act, denev_v[jj]),), incs=((pe, 1),))
                    dcol_v[jj] = pe.count

                    sc.add("vector", lambda E, j=jj: E.reciprocal(
                        dinv_all[:, 2 * j:2 * j + 2], ps_den[:, QC:QC + 2]),
                        waits=((pe, dcol_v[jj]),), incs=((dve, 1),))
                    dinv_v[jj] = dve.count

                    def cst(E, j=jj):
                        E.dma_start(out=ctx_scr[0:4, :, j * QC:(j + 1) * QC].rearrange("cc p t -> p cc t"),
                                        in_=ctxout[j & 1][:, 0:4, :]).then_inc(st.h, 16)
                        return E.dma_start(out=ctx_scr[4:8, :, j * QC:(j + 1) * QC].rearrange("cc p t -> p cc t"),
                                               in_=ctxout[j & 1][:, 4:8, :])
                    sc.add("gpsimd", cst, waits=((dve, ctxev_v[jj]),), incs=((st, 16),))
                    st.count += 16
                    ctxst_v[jj] = st.count

                if j >= 1:
                    emit_tail_pe(j - 1)

                def qn_slack(E):
                    E.matmul(ps_st[0][:, 0:128], onesrf[:], onesrf[:], start=True, stop=True)
                    return E.matmul(ps_st[0][:, 0:128], onesrf[:], onesrf[:], start=True, stop=True)
                sc.add("tensor", qn_slack, incs=((pe, 1),))
                qnsl_v = pe.count

                emit_vload(0)
                emit_st(0)
                emit_st(1)

                sc.add("vector", lambda E: E.reduce_max(cs[0:1, 0:1], ps_misc[0:1, QC:2 * QC], axis=AX.X),
                       waits=((pe, qnsl_v),), incs=((dve, 1),))
                sc.add("vector", lambda E: E.tensor_mul(cs[0:1, 1:2], cs[0:1, 0:1], ck2[:]),
                       waits=((dve, dve.count),), incs=((dve, 1),))
                c2_v = dve.count
                if _DBG:
                    def _dn0(E, j=j):
                        E.tensor_copy(negs_sb[0:1, j:j + 1], cs[0:1, 0:1])
                        return E.tensor_copy(negs_sb[0:1, 16 + j:17 + j], cs[0:1, 1:2])
                    sc.add("vector", _dn0, incs=((dve, 1),))

                sc.add("scalar", lambda E: E.activation(cs[0:1, 2:3], cs[0:1, 1:2], AF.Ln),
                       waits=((dve, c2_v),), incs=((act, 1),))
                sc.add("scalar", lambda E: E.activation(cs[0:1, 3:4], cs[0:1, 2:3], AF.Exp, scale=0.5),
                       waits=((act, act.count),), incs=((act, 1),))
                cexp_v = act.count
                if _DBG:
                    sc.add("scalar", lambda E, j=j: E.copy(negs_sb[0:1, 32 + j:33 + j], cs[0:1, 3:4]),
                           incs=((act, 1),))

                sc.add("vector", lambda E: E.tensor_scalar(cs[0:1, 4:5], cs[0:1, 3:4], -1.02, None, ALU.mult),
                       waits=((act, cexp_v),), incs=((dve, 1),))
                negneg_v[j] = dve.count

                def bc(E):
                    E.matmul(ps_misc[:, 300:301], onesrf[:], cs[0:1, 4:5], start=True, stop=True)
                    E.matmul(ps_misc[:, 310:438], onesrf[:], onesrf[:], start=True, stop=True)
                    return E.matmul(ps_misc[:, 310:438], onesrf[:], onesrf[:], start=True, stop=True)
                sc.add("tensor", bc, waits=((dve, negneg_v[j]),), incs=((pe, 1),))
                bc_v = pe.count
                sc.add("scalar", lambda E: E.copy(negc[:], ps_misc[:, 300:301]),
                       waits=((pe, bc_v),), incs=((act, 1),))
                negcp_v[j] = act.count
                if _DBG:
                    sc.add("vector", lambda E, j=j: E.tensor_copy(
                        negs_sb[0:1, 48 + j:49 + j], cs[0:1, 4:5]),
                        waits=((dve, negneg_v[j]),), incs=((dve, 1),))

                emit_exp(0)
                for kt in range(2, NKT):
                    if kt % 4 == 0:
                        emit_vload(kt)
                    emit_st(kt)
                    emit_exp(kt - 1)
                    emit_pv(kt - 2)
                emit_exp(NKT - 1)
                emit_pv(NKT - 2)
                emit_pv(NKT - 1)

                if j == NQ - 1:
                    emit_tail_act(j)
                    def last_slack(E):
                        E.matmul(ps_misc[:, 310:438], onesrf[:], onesrf[:], start=True, stop=True)
                        return E.matmul(ps_misc[:, 310:438], onesrf[:], onesrf[:], start=True, stop=True)
                    sc.add("tensor", last_slack, incs=((pe, 1),))
                    emit_denev(j, pe.count)
                    emit_tail_pe(j)

            add_barrier(sc, sm)
            sc.emit(nc, "attn")

        mctx.close()

        with ExitStack() as ctx:
            sb = lambda *a: ctx.enter_context(nc.sbuf_tensor(*a))
            ps = lambda *a: ctx.enter_context(nc.psum_tensor(*a))
            woT = sb("woT", [128, CCN, C], BF16)
            wrT = sb("wrT", [128, CCN, C], BF16)
            wkfT = sb("wkfT", [128, CCN, C], BF16)
            wvfT = sb("wvfT", [128, CCN, C], BF16)
            xnt_d = [sb(f"xnt_d{i}", [128, CCN, G + 1], BF16) for i in range(2)]
            xin_d = [sb(f"xin_d{i}", [128, 4, C], F32) for i in range(2)]
            ctx_d = [sb(f"ctx_d{i}", [128, CCN, G], BF16) for i in range(2)]
            xr = sb("xr", [128, CCN, G], BF16)
            xk = sb("xk", [128, CCN, G], BF16)
            dtmp = sb("dtmp", [128, G], BF16)
            kact = sb("kact", [128, CCN, G], BF16)
            krelu = [sb(f"krelu{i}", [128, G], F32) for i in range(2)]
            rnat = sb("rnat", [128, 4, C], BF16)
            t1 = sb("t1", [128, C], F32)
            t2 = sb("t2", [128, C], F32)
            outst = sb("outst", [128, 4, C], F32)
            ps_kf = [ps(f"ps_kf{i}", [128, 512], F32) for i in range(2)]
            ps_r = ps("ps_r", [128, 1024], F32)
            ps_vf = ps("ps_vf", [128, 1024], F32)
            ps_ao = ps("ps_ao", [128, 1024], F32)

            sc = Sched()
            wdld_v = {}
            for dst, m in ((wkfT, 5), (wrT, 4), (wvfT, 6), (woT, 3)):
                for hh in range(2):
                    sc.add("gpsimd", lambda E, d=dst, m=m, hh=hh: E.dma_start(
                        out=d[:, hh * 4:(hh + 1) * 4, :],
                        in_=wscr[m, hh * 4:(hh + 1) * 4, :, :].rearrange("cc p d -> p cc d")),
                        incs=((st, 16),))
                wdld_v[m] = st.count

            xntd_v = {}; xind_v = {}; ctxd_v = {}
            mix_v = {}; mixk_v = {}
            kfmm_v = {}; relu_v = {}; sqd_v = {}
            rmm_v = {}; sig_v = {}; vfmm_v = {}; aomm_v = {}
            comb_v = {}; outst_v = {}

            def emit_loads(d):
                w = []
                if d >= 2:
                    w.append((dve, mix_v[d - 2]))

                def xntl(E, d=d):
                    if d == 0:
                        E.dma_start(out=xnt_d[0][:, 0:4, 1:G + 1],
                                    in_=xnt_scr[0:4, :, 0:G].rearrange("cc p t -> p cc t")
                                    ).then_inc(ld.h, 16)
                        return E.dma_start(out=xnt_d[0][:, 4:8, 1:G + 1],
                                           in_=xnt_scr[4:8, :, 0:G].rearrange("cc p t -> p cc t"))
                    lo = d * G - 1
                    E.dma_start(out=xnt_d[d & 1][:, 0:4, :],
                                in_=xnt_scr[0:4, :, lo:lo + G + 1].rearrange("cc p t -> p cc t")
                                ).then_inc(ld.h, 16)
                    return E.dma_start(out=xnt_d[d & 1][:, 4:8, :],
                                       in_=xnt_scr[4:8, :, lo:lo + G + 1].rearrange("cc p t -> p cc t"))
                sc.add("sync", xntl, waits=tuple(w), incs=((ld, 16),))
                ld.count += 16
                xntd_v[d] = ld.count

                w = []
                if d >= 2:
                    w.append((dve, comb_v[(d - 2, 3)]))
                sc.add("sync", lambda E, d=d: E.dma_start(
                    out=xin_d[d & 1][:],
                    in_=x_d[d * G:(d + 1) * G, :].rearrange("(i p) c -> p i c", p=128)),
                    waits=tuple(w), incs=((ld, 16),))
                xind_v[d] = ld.count

                w = []
                if d >= 2:
                    w.append((pe, aomm_v[(d - 2, 3)]))

                def ctxl(E, d=d):
                    E.dma_start(out=ctx_d[d & 1][:, 0:4, :],
                                in_=ctx_scr[0:4, :, d * G:(d + 1) * G].rearrange("cc p t -> p cc t")
                                ).then_inc(ld.h, 16)
                    return E.dma_start(out=ctx_d[d & 1][:, 4:8, :],
                                       in_=ctx_scr[4:8, :, d * G:(d + 1) * G].rearrange("cc p t -> p cc t"))
                sc.add("sync", ctxl, waits=tuple(w), incs=((ld, 16),))
                ld.count += 16
                ctxd_v[d] = ld.count

            def emit_mixk(d):
                def mixes_k(E, d=d):
                    inst = None
                    for cc in range(CCN):
                        E.tensor_sub(dtmp[:], xnt_d[d & 1][:, cc, 1:G + 1], xnt_d[d & 1][:, cc, 0:G])
                        inst = E.scalar_tensor_tensor(xk[:, cc, :], dtmp[:], muk_sb[:, cc:cc + 1],
                                                      xnt_d[d & 1][:, cc, 0:G], ALU.mult, ALU.add)
                    return inst
                w = [(ld, xntd_v[d])]
                if d == 0:
                    w.append((dve, dve.count))
                if d >= 1:
                    w.append((pe, kfmm_v[(d - 1, CCN - 1)]))
                sc.add("vector", mixes_k, waits=tuple(w), incs=((dve, 1),))
                mixk_v[d] = dve.count

            def emit_mixr(d):
                def mixes_r(E, d=d):
                    inst = None
                    for cc in range(CCN):
                        E.tensor_sub(dtmp[:], xnt_d[d & 1][:, cc, 1:G + 1], xnt_d[d & 1][:, cc, 0:G])
                        inst = E.scalar_tensor_tensor(xr[:, cc, :], dtmp[:], mur_sb[:, cc:cc + 1],
                                                      xnt_d[d & 1][:, cc, 0:G], ALU.mult, ALU.add)
                    return inst
                w = [(ld, xntd_v[d])]
                if d >= 1:
                    w.append((pe, rmm_v[(d - 1, 3)]))
                sc.add("vector", mixes_r, waits=tuple(w), incs=((dve, 1),))
                mix_v[d] = dve.count

            emit_loads(0)
            sc.add("vector", lambda E: E.tensor_copy(xnt_d[0][:, :, 0:1], xnhT[:].unsqueeze(2)),
                   waits=((ld, xntd_v[0]),), incs=((dve, 1),))
            emit_mixk(0)
            emit_mixr(0)

            for d in range(ND):
                for dc in range(CCN):
                    def kfm(E, d=d, dc=dc):
                        inst = None
                        for cc in range(CCN):
                            inst = E.matmul(ps_kf[dc & 1][:],
                                            wkfT[:, cc, dc * 128:(dc + 1) * 128],
                                            xk[:, cc, :],
                                            start=(cc == 0), stop=(cc == CCN - 1))
                        return inst
                    w = []
                    if dc >= 2:
                        w.append((act, relu_v[(d, dc - 2)]))
                    if dc == 0:
                        w.append((dve, mixk_v[d]))
                        if d == 0:
                            w.append((st, wdld_v[5]))
                    sc.add("tensor", kfm, waits=tuple(w), incs=((pe, 1),))
                    kfmm_v[(d, dc)] = pe.count

                    w = [(pe, kfmm_v[(d, dc)])]
                    if d * 8 + dc >= 2:
                        pi = d * 8 + dc - 2
                        w.append((dve, sqd_v[(pi // 8, pi % 8)]))
                    sc.add("scalar", lambda E, d=d, dc=dc: E.activation(
                        krelu[dc & 1][:], ps_kf[dc & 1][:], AF.Relu),
                        waits=tuple(w), incs=((act, 1),))
                    relu_v[(d, dc)] = act.count

                    w = [(act, relu_v[(d, dc)])]
                    if d >= 1 and dc == 0:
                        w.append((pe, vfmm_v[(d - 1, 3)]))
                    sc.add("vector", lambda E, d=d, dc=dc: E.tensor_mul(
                        kact[:, dc, :], krelu[dc & 1][:], krelu[dc & 1][:]),
                        waits=tuple(w), incs=((dve, 1),))
                    sqd_v[(d, dc)] = dve.count

                if d + 1 < ND:
                    emit_loads(d + 1)
                    emit_mixk(d + 1)

                for tt in range(4):
                    w = []
                    if tt == 0:
                        w.append((dve, mix_v[d]))
                    prev = (d, tt - 1) if tt >= 1 else (d - 1, 3)
                    if prev[0] >= 0:
                        w.append((act, sig_v[prev]))
                        if tt == 0:
                            w.append((st, wdld_v[4]))

                    def rmm(E, d=d, tt=tt):
                        inst = None
                        for cc in range(CCN):
                            for nh in range(2):
                                inst = E.matmul(ps_r[:, nh * 512:(nh + 1) * 512],
                                                xr[:, cc, tt * 128:(tt + 1) * 128],
                                                wrT[:, cc, nh * 512:(nh + 1) * 512],
                                                start=(cc == 0), stop=(cc == CCN - 1))
                        return inst
                    if d == 0 and tt == 0:
                        w.append((st, wdld_v[4]))
                    sc.add("tensor", rmm, waits=tuple(w), incs=((pe, 1),))
                    rmm_v[(d, tt)] = pe.count
                    if tt == 3 and d + 1 < ND:
                        emit_mixr(d + 1)

                    w = [(pe, rmm_v[(d, tt)])]
                    if d >= 1 and tt == 0:
                        w.append((dve, comb_v[(d - 1, 3)]))
                    sc.add("scalar", lambda E, d=d, tt=tt: E.activation(
                        rnat[:, tt, :], ps_r[:], AF.Sigmoid),
                        waits=tuple(w), incs=((act, 1),))
                    sig_v[(d, tt)] = act.count

                    w = [(dve, sqd_v[(d, CCN - 1)])]
                    prev = (d, tt - 1) if tt >= 1 else (d - 1, 3)
                    if prev[0] >= 0:
                        w.append((dve, comb_v[prev]))
                    if d == 0 and tt == 0:
                        w.append((st, wdld_v[6]))

                    def vfm(E, d=d, tt=tt):
                        inst = None
                        for dc in range(CCN):
                            for nh in range(2):
                                inst = E.matmul(ps_vf[:, nh * 512:(nh + 1) * 512],
                                                kact[:, dc, tt * 128:(tt + 1) * 128],
                                                wvfT[:, dc, nh * 512:(nh + 1) * 512],
                                                start=(dc == 0), stop=(dc == CCN - 1))
                        return inst
                    sc.add("tensor", vfm, waits=tuple(w), incs=((pe, 1),))
                    vfmm_v[(d, tt)] = pe.count

                    w = [(ld, ctxd_v[d])]
                    if prev[0] >= 0:
                        w.append((dve, comb_v[prev]))
                    if d == 0 and tt == 0:
                        w.append((st, wdld_v[3]))

                    def aom(E, d=d, tt=tt):
                        inst = None
                        for dc in range(CCN):
                            for nh in range(2):
                                inst = E.matmul(ps_ao[:, nh * 512:(nh + 1) * 512],
                                                ctx_d[d & 1][:, dc, tt * 128:(tt + 1) * 128],
                                                woT[:, dc, nh * 512:(nh + 1) * 512],
                                                start=(dc == 0), stop=(dc == CCN - 1))
                        return inst
                    sc.add("tensor", aom, waits=tuple(w), incs=((pe, 1),))
                    aomm_v[(d, tt)] = pe.count

                    w = [(pe, aomm_v[(d, tt)]), (pe, vfmm_v[(d, tt)]),
                         (act, sig_v[(d, tt)]), (ld, xind_v[d])]
                    if d >= 1 and tt == 0:
                        w.append((st, outst_v[d - 1]))

                    def comb(E, d=d, tt=tt):
                        E.scalar_tensor_tensor(t1[:], ps_ao[:], dinv_all[:, 4 * d + tt:4 * d + tt + 1],
                                               xin_d[d & 1][:, tt, :], ALU.mult, ALU.add)
                        E.tensor_mul(t2[:], ps_vf[:], rnat[:, tt, :])
                        return E.tensor_add(outst[:, tt, :], t1[:], t2[:])
                    sc.add("vector", comb, waits=tuple(w), incs=((dve, 1),))
                    comb_v[(d, tt)] = dve.count

                sc.add("gpsimd", lambda E, d=d: E.dma_start(
                    out=y_d[d * G:(d + 1) * G, :].rearrange("(i p) c -> p i c", p=128),
                    in_=outst[:]),
                    waits=((dve, comb_v[(d, 3)]),), incs=((st, 16),))
                outst_v[d] = st.count

            if _DBG:
                for dst, src in ((dbg_dinv, dinv_all), (dbg_negc, negc),
                                 (dbg_cs, cs), (dbg_ck2, ck2), (dbg_xnhT, xnhT),
                                 (dbg_negs, negs_sb), (dbg_rstd, rstdall),
                                 (dbg_mean, meanall)):
                    sc.add("sync", lambda E, d=dst, s=src: E.dma_start(out=d, in_=s[:]),
                           incs=((st, 16),))

            sc.add("gpsimd", None, waits=((st, st.count),))
            sc.emit(nc, "final")

    return nc


_NC_CACHE = None


def kernel(L0_attn_x, L0_attn_W_qkv, L0_attn_W_out, L0_ffn_W_r, L0_ffn_W_k,
           L0_ffn_W_v, L0_ffn_mu_r, L0_ffn_mu_k):
    global _NC_CACHE
    x = np.asarray(L0_attn_x, dtype=np.float32)
    wqkv = np.ascontiguousarray(np.asarray(L0_attn_W_qkv, dtype=np.float32))
    wout = np.ascontiguousarray(np.asarray(L0_attn_W_out, dtype=np.float32))
    wr = np.ascontiguousarray(np.asarray(L0_ffn_W_r, dtype=np.float32))
    wkf = np.ascontiguousarray(np.asarray(L0_ffn_W_k, dtype=np.float32))
    wvf = np.ascontiguousarray(np.asarray(L0_ffn_W_v, dtype=np.float32))
    mus = np.ascontiguousarray(
        np.concatenate([np.asarray(L0_ffn_mu_r, dtype=np.float32).reshape(-1),
                        np.asarray(L0_ffn_mu_k, dtype=np.float32).reshape(-1)]).reshape(1, 2 * C))

    consts = {
        "identb": np.eye(128, dtype=ml_dtypes.bfloat16),
        "identf": np.eye(128, dtype=np.float32),
        "onesb": np.ones((128, 1), dtype=ml_dtypes.bfloat16),
        "ones1b": np.ones((1, 1), dtype=ml_dtypes.bfloat16),
        "onesrf": np.ones((1, 128), dtype=np.float32),
        "onescf": np.ones((128, 1), dtype=np.float32),
        "ones1f": np.ones((1, 1), dtype=np.float32),
    }

    B = x.shape[0]
    in_maps = []
    for core in range(8):
        b, h = core // 2, core % 2
        own = x[b, h * OWN:(h + 1) * OWN]
        oth = x[b, (1 - h) * OWN:(2 - h) * OWN]
        xp = np.ascontiguousarray(np.concatenate([own, oth], axis=0))
        xh = (np.zeros((1, C), np.float32) if h == 0
              else np.ascontiguousarray(x[b, OWN - 1:OWN]))
        in_maps.append({
            "x": xp, "xh": xh, "wqkv": wqkv, "wout": wout,
            "wr": wr, "wkf": wkf, "wvf": wvf, "mus": mus, **consts,
        })

    if _NC_CACHE is None:
        _NC_CACHE = build_program()
    res = run_bass_kernel_spmd(_NC_CACHE, in_maps, core_ids=list(range(8)))

    out = np.empty((B, S, C), dtype=np.float32)
    for core in range(8):
        b, h = core // 2, core % 2
        out[b, h * OWN:(h + 1) * OWN] = res.results[core]["y"]
    return out


if __name__ == "__main__":
    nc = build_program()
    print("built ok; instructions:", nc.n_instructions())


# version 32
# speedup vs baseline: 1.0162x; 1.0162x over previous
import os
import numpy as np
import ml_dtypes
from contextlib import ExitStack

_DBG = bool(os.environ.get("KDBG"))

import concourse.bass as bass
import concourse.mybir as mybir
from concourse.bass_utils import run_bass_kernel_spmd

F32 = mybir.dt.float32
BF16 = mybir.dt.bfloat16
AF = mybir.ActivationFunctionType
ALU = mybir.AluOpType
AX = mybir.AxisListType

ENGS = ("sync", "tensor", "vector", "scalar", "gpsimd")

S = 8192
OWN = 4096
C = 1024
CCN = 8
G = 512
NG = S // G
QC = 256
NQ = OWN // QC
NKT = S // 128
ND = OWN // G
EPS = 1e-5


class Sem:
    def __init__(self, nc, ctx, name):
        self.h = ctx.enter_context(nc.semaphore(name))
        self.count = 0


class Sched:

    def __init__(self):
        self.items = {e: [] for e in ENGS}

    def add(self, eng, fn=None, waits=(), incs=()):
        for s, n in incs:
            s.count += n
        self.items[eng].append((fn, tuple(waits), tuple(incs)))

    def emit(self, nc, name):
        with nc.Block(name) as block:
            for eng in ENGS:
                items = self.items[eng]

                def body(E, items=items):
                    for fn, waits, incs in items:
                        for s, v in waits:
                            E.wait_ge(s.h, v)
                        inst = fn(E) if fn is not None else None
                        for s, n in incs:
                            if inst is None:
                                E.sem_inc(s.h, n)
                            else:
                                inst = inst.then_inc(s.h, n)

                getattr(block, eng)(body)


def add_barrier(sc, sm):
    sc.add("sync", None, waits=((sm["ld"], sm["ld"].count),))
    sc.add("gpsimd", None, waits=((sm["st"], sm["st"].count),))
    sc.add("tensor", None, waits=((sm["pe"], sm["pe"].count),))
    sc.add("vector", None, waits=((sm["dve"], sm["dve"].count),))
    sc.add("scalar", None, waits=((sm["act"], sm["act"].count),))
    for e in ENGS:
        sc.add(e, None, incs=((sm["ph"], 1),))
    tgt = sm["ph"].count
    for e in ENGS:
        sc.add(e, None, waits=((sm["ph"], tgt),))


def build_program():
    nc = bass.Bass("TRN2", target_bir_lowering=False)

    x_d = nc.dram_tensor("x", [S, C], F32, kind="ExternalInput").ap()
    xh_d = nc.dram_tensor("xh", [1, C], F32, kind="ExternalInput").ap()
    wqkv_d = nc.dram_tensor("wqkv", [3 * C, C], F32, kind="ExternalInput").ap()
    wout_d = nc.dram_tensor("wout", [C, C], F32, kind="ExternalInput").ap()
    wr_d = nc.dram_tensor("wr", [C, C], F32, kind="ExternalInput").ap()
    wkf_d = nc.dram_tensor("wkf", [C, C], F32, kind="ExternalInput").ap()
    wvf_d = nc.dram_tensor("wvf", [C, C], F32, kind="ExternalInput").ap()
    mus_d = nc.dram_tensor("mus", [1, 2 * C], F32, kind="ExternalInput").ap()
    identb_d = nc.dram_tensor("identb", [128, 128], BF16, kind="ExternalInput").ap()
    identf_d = nc.dram_tensor("identf", [128, 128], F32, kind="ExternalInput").ap()
    onesb_d = nc.dram_tensor("onesb", [128, 1], BF16, kind="ExternalInput").ap()
    ones1b_d = nc.dram_tensor("ones1b", [1, 1], BF16, kind="ExternalInput").ap()
    onesrf_d = nc.dram_tensor("onesrf", [1, 128], F32, kind="ExternalInput").ap()
    onescf_d = nc.dram_tensor("onescf", [128, 1], F32, kind="ExternalInput").ap()
    ones1f_d = nc.dram_tensor("ones1f", [1, 1], F32, kind="ExternalInput").ap()

    y_d = nc.dram_tensor("y", [OWN, C], F32, kind="ExternalOutput").ap()

    _skind = "ExternalOutput" if _DBG else "Internal"
    wscr = nc.dram_tensor("wscr", [7, CCN, 128, C], BF16, kind=_skind).ap()
    xnt_scr = nc.dram_tensor("xnt_scr", [CCN, 128, S], BF16, kind=_skind).ap()
    v_scr = nc.dram_tensor("v_scr", [S, C], BF16, kind=_skind).ap()
    ctx_scr = nc.dram_tensor("ctx_scr", [CCN, 128, OWN], BF16, kind=_skind).ap()
    if _DBG:
        dbg_dinv = nc.dram_tensor("dbg_dinv", [128, 32], F32, kind="ExternalOutput").ap()
        dbg_negc = nc.dram_tensor("dbg_negc", [128, 1], F32, kind="ExternalOutput").ap()
        dbg_cs = nc.dram_tensor("dbg_cs", [1, 8], F32, kind="ExternalOutput").ap()
        dbg_ck2 = nc.dram_tensor("dbg_ck2", [1, 1], F32, kind="ExternalOutput").ap()
        dbg_xnhT = nc.dram_tensor("dbg_xnhT", [128, CCN], BF16, kind="ExternalOutput").ap()
        dbg_negs = nc.dram_tensor("dbg_negs", [1, 64], F32, kind="ExternalOutput").ap()
        dbg_rstd = nc.dram_tensor("dbg_rstd", [128, 64], F32, kind="ExternalOutput").ap()
        dbg_mean = nc.dram_tensor("dbg_mean", [128, 64], F32, kind="ExternalOutput").ap()
        dbg_ssq = nc.dram_tensor("dbg_ssq", [128, 64], F32, kind="ExternalOutput").ap()
        dbg_tmp4 = nc.dram_tensor("dbg_tmp4", [128, 64], F32, kind="ExternalOutput").ap()

    with ExitStack() as octx:
        sm = {n: Sem(nc, octx, n) for n in ("ld", "st", "pe", "act", "dve", "ph")}
        ld, st, pe, act, dve = sm["ld"], sm["st"], sm["pe"], sm["act"], sm["dve"]

        osb = lambda *a: octx.enter_context(nc.sbuf_tensor(*a))
        identb = osb("identb_s", [128, 128], BF16)
        identf = osb("identf_s", [128, 128], F32)
        onesb = osb("onesb_s", [128, 1], BF16)
        ones1b = osb("ones1b_s", [1, 1], BF16)
        onesrf = osb("onesrf_s", [1, 128], F32)
        onescf = osb("onescf_s", [128, 1], F32)
        ones1f = osb("ones1f_s", [1, 1], F32)
        mur_sb = osb("mur_sb", [128, CCN], F32)
        muk_sb = osb("muk_sb", [128, CCN], F32)
        ck2 = osb("ck2", [1, 1], F32)
        xnhT = osb("xnhT", [128, CCN], BF16)
        dinv_all = osb("dinv_all", [128, 32], F32)
        negc = osb("negc", [128, 1], F32)
        den_sb = osb("den_sb", [1, 256], F32)
        negs_sb = osb("negs_sb", [1, 64], F32) if _DBG else None
        cs = osb("cs", [1, 8], F32)
        dmy = osb("dmy", [128, 8], F32)
        dmyr = osb("dmyr", [1, 256], F32)
        hstat = osb("hstat", [1, 8], F32)
        meanall = osb("meanall", [128, 64], F32)
        rstdall = osb("rstdall", [128, 64], F32)

        with ExitStack() as ctx:
            sb = lambda *a: ctx.enter_context(nc.sbuf_tensor(*a))
            ps = lambda *a: ctx.enter_context(nc.psum_tensor(*a))
            wtmp = [sb(f"wtmp{i}", [128, CCN, C], F32) for i in range(3)]
            wstage = [sb(f"wstage{i}", [128, CCN, 128], BF16) for i in range(2)]
            murow = sb("murow", [1, 2 * C], F32)
            hsq = sb("hsq", [1, C], F32)
            xh_p = sb("xh_p", [1, C], F32)
            xnh_row = sb("xnh_row", [1, C], BF16)
            ps_w = [ps(f"ps_w{i}", [128, 512], F32) for i in range(4)]
            ps_mu = ps("ps_mu", [128, 512], F32)

            sc = Sched()
            for dst, src in ((identb, identb_d), (identf, identf_d), (onesb, onesb_d),
                             (ones1b, ones1b_d), (onesrf, onesrf_d), (ones1f, ones1f_d),
                             (murow, mus_d), (onescf, onescf_d)):
                sc.add("sync", lambda E, d=dst, s=src: E.dma_start(out=d[:], in_=s),
                       incs=((ld, 16),))
            consts_ld = ld.count

            def mu_mms(E):
                inst = None
                for m in range(2):
                    for cc in range(CCN):
                        i = m * CCN + cc
                        inst = E.matmul(ps_mu[:, i:i + 1],
                                        murow[0:1, m * C + cc * 128: m * C + (cc + 1) * 128],
                                        ones1f[:], start=True, stop=True)
                return inst
            sc.add("tensor", mu_mms, waits=((ld, consts_ld),), incs=((pe, 1),))
            mu_pe = pe.count
            sc.add("scalar", lambda E: E.copy(mur_sb[:], ps_mu[:, 0:CCN]),
                   waits=((pe, mu_pe),))
            sc.add("scalar", lambda E: E.copy(muk_sb[:], ps_mu[:, CCN:2 * CCN]),
                   incs=((act, 1),))
            mucp_v = act.count

            wqkv3 = wqkv_d.rearrange("(h three) c -> three h c", three=3)
            srcs = [wqkv3[0], wqkv3[1], wqkv3[2], wout_d, wr_d, wkf_d, wvf_d]

            cast_v = {}; tp_v = {}; stg_v = {}; sto_v = {}
            wld_v = {}
            for m in range(7):
                w = [(pe, tp_v[(m - 3) * CCN + CCN - 1])] if m >= 3 else []

                def wld(E, m=m):
                    E.dma_start(out=wtmp[m % 3][:, 0:4, :],
                                in_=srcs[m][0:512, :].rearrange("(i p) c -> p i c", p=128)
                                ).then_inc(ld.h, 16)
                    return E.dma_start(out=wtmp[m % 3][:, 4:8, :],
                                       in_=srcs[m][512:1024, :].rearrange("(i p) c -> p i c", p=128))
                sc.add("sync", wld, waits=tuple(w), incs=((ld, 16),))
                ld.count += 16
                wld_v[m] = ld.count

                for dblk in range(CCN):
                    it = m * CCN + dblk
                    r = it & 1

                    def tmm(E, m=m, dblk=dblk, r=r):
                        inst = None
                        for cc in range(CCN):
                            inst = E.matmul(ps_w[r * 2 + cc // 4][:, (cc % 4) * 128:(cc % 4 + 1) * 128],
                                            wtmp[m % 3][:, dblk, cc * 128:(cc + 1) * 128],
                                            identf[:], start=True, stop=True)
                        return inst
                    w = [(ld, wld_v[m] - (16 if dblk < 4 else 0))] + \
                        ([(act, stg_v[it - 2])] if it >= 2 else [])
                    sc.add("tensor", tmm, waits=tuple(w), incs=((pe, 1),))
                    tp_v[it] = pe.count
                    cast_v[it] = 0

                    def stg(E, r=r):
                        E.copy(wstage[r][:, 0:4, :], ps_w[r * 2][:].rearrange("p (i c) -> p i c", i=4))
                        return E.copy(wstage[r][:, 4:8, :], ps_w[r * 2 + 1][:].rearrange("p (i c) -> p i c", i=4))
                    w = [(pe, tp_v[it])] + ([(st, sto_v[it - 2])] if it >= 2 else [])
                    sc.add("scalar", stg, waits=tuple(w), incs=((act, 1),))
                    stg_v[it] = act.count

                    def sto(E, m=m, dblk=dblk, r=r):
                        E.dma_start(
                            out=wscr[m, 0:4, :, dblk * 128:(dblk + 1) * 128].rearrange("cc p d -> p cc d"),
                            in_=wstage[r][:, 0:4, :]).then_inc(st.h, 16)
                        return E.dma_start(
                            out=wscr[m, 4:8, :, dblk * 128:(dblk + 1) * 128].rearrange("cc p d -> p cc d"),
                            in_=wstage[r][:, 4:8, :])
                    sc.add("gpsimd", sto, waits=((act, stg_v[it]),), incs=((st, 16),))
                    st.count += 16
                    sto_v[it] = st.count

            sc.add("sync", lambda E: E.dma_start(out=xh_p[:], in_=xh_d), incs=((ld, 16),))
            xh_ld = ld.count
            sc.add("vector", lambda E: E.reduce_sum(hstat[0:1, 0:1], xh_p[:], axis=AX.X),
                   waits=((ld, xh_ld),))
            sc.add("vector", lambda E: E.tensor_mul(hsq[:], xh_p[:], xh_p[:]),
                   waits=((ld, xh_ld),), incs=((dve, 1),))
            sc.add("vector", lambda E: E.reduce_sum(hstat[0:1, 1:2], hsq[:], axis=AX.X),
                   waits=((dve, dve.count),), incs=((dve, 1),))
            sc.add("vector", lambda E: E.tensor_scalar(hstat[0:1, 2:3], hstat[0:1, 0:1], 1.0 / C, None, ALU.mult),
                   waits=((dve, dve.count),), incs=((dve, 1),))
            sc.add("vector", lambda E: E.tensor_scalar(hstat[0:1, 3:4], hstat[0:1, 1:2], 1.0 / C, EPS, ALU.mult, ALU.add),
                   incs=((dve, 1),))
            sc.add("vector", lambda E: E.tensor_mul(hstat[0:1, 4:5], hstat[0:1, 2:3], hstat[0:1, 2:3]),
                   waits=((dve, dve.count),), incs=((dve, 1),))
            sc.add("vector", lambda E: E.tensor_sub(hstat[0:1, 5:6], hstat[0:1, 3:4], hstat[0:1, 4:5]),
                   waits=((dve, dve.count),), incs=((dve, 1),))
            hv = dve.count
            sc.add("scalar", lambda E: E.activation(hstat[0:1, 6:7], hstat[0:1, 5:6], AF.Ln),
                   waits=((dve, hv),), incs=((act, 1),))
            sc.add("scalar", lambda E: E.activation(hstat[0:1, 7:8], hstat[0:1, 6:7], AF.Exp, scale=-0.5),
                   waits=((act, act.count),), incs=((act, 1),))
            hv2 = act.count
            sc.add("vector", lambda E: E.tensor_scalar(
                xnh_row[:], xh_p[:], hstat[0:1, 2:3], hstat[0:1, 7:8],
                ALU.subtract, ALU.mult),
                waits=((act, hv2),), incs=((dve, 1),))
            hv4 = dve.count

            def hT(E):
                for cc in range(CCN):
                    E.matmul(ps_mu[:, 16 + cc:17 + cc],
                             xnh_row[0:1, cc * 128:(cc + 1) * 128],
                             ones1b[:], start=True, stop=True)
                E.matmul(ps_mu[:, 32:160], onesrf[:], onesrf[:], start=True, stop=True)
                return E.matmul(ps_mu[:, 32:160], onesrf[:], onesrf[:], start=True, stop=True)
            sc.add("tensor", hT, waits=((dve, hv4), (act, mucp_v)), incs=((pe, 1),))
            hv5 = pe.count
            sc.add("scalar", lambda E: E.copy(xnhT[:], ps_mu[:, 16:16 + CCN]),
                   waits=((pe, hv5),), incs=((act, 1),))

            add_barrier(sc, sm)
            sc.emit(nc, "p0")

        mctx = ExitStack()
        ktres = mctx.enter_context(nc.sbuf_tensor("ktres", [128, CCN, S], BF16))

        with ExitStack() as ctx:
            sb = lambda *a: ctx.enter_context(nc.sbuf_tensor(*a))
            ps = lambda *a: ctx.enter_context(nc.psum_tensor(*a))
            wkT = sb("wkT", [128, CCN, C], BF16)
            wvT = sb("wvT", [128, CCN, C], BF16)
            xin = [sb(f"xin{i}", [128, 2, C], F32) for i in range(2)]
            xnb = sb("xnb", [128, 4, C], BF16)
            xnT = sb("xnT", [128, CCN, G], BF16)
            knsq = [sb(f"knsq{i}", [128, G], BF16) for i in range(2)]
            vout = [sb(f"vout{i}", [128, 2, C], BF16) for i in range(2)]
            bnraw = sb("bnraw", [128, 2, 2, 6], F32)
            bnmv = sb("bnmv", [128, 2, 2], F32)
            tmp2 = sb("tmp2", [128, 2], F32)
            lnv = sb("lnv", [128, 2], F32)
            ps_t = [ps(f"ps_t{i}", [128, 512], F32) for i in range(4)]
            ps_kt = [ps(f"ps_kt{i}", [128, 512], F32) for i in range(2)]
            ps_kn = ps("ps_kn", [128, 512], F32)
            ps_v = ps("ps_v", [128, 512], F32)

            sc = Sched()

            ld_hg = {}
            apply_v = {}
            tp_v = {}
            evac_v = {}
            ktmm_v = {}
            ktev_v = {}
            knsq_v = {}
            knmm_v = {}
            knrd_v = {}
            xnst_v = {}
            vmm_v = {}
            vev_v = {}
            vst_v = {}

            def emit_xload(g):
                for half in range(2):
                    hg = 2 * g + half
                    w = []
                    if hg >= 2:
                        gp, hp = (hg - 2) // 2, (hg - 2) % 2
                        w.append((dve, apply_v[(gp, hp * 2 + 1)]))
                    sc.add("sync", lambda E, hg=hg: E.dma_start(
                        out=xin[hg & 1][:],
                        in_=x_d[hg * 256:(hg + 1) * 256, :].rearrange("(i p) c -> p i c", p=128)),
                        waits=tuple(w), incs=((ld, 16),))
                    ld_hg[hg] = ld.count

            ln_v = {}
            rstd_v = {}

            def emit_stats_chunk(hg, j):
                sc.add("vector", lambda E, hg=hg, j=j: E.bn_stats(
                    bnraw[:, j // 2, j % 2, :],
                    xin[hg & 1][:, j // 2, (j % 2) * 512:(j % 2 + 1) * 512]),
                    waits=((ld, ld_hg[hg]),), incs=((dve, 1),))

            def emit_stats_tail(hg):
                sc.add("vector", lambda E: [E.bn_aggr(bnmv[:, 0, :], bnraw[:, 0, :, :]),
                                            E.bn_aggr(bnmv[:, 1, :], bnraw[:, 1, :, :])][-1],
                       waits=((dve, dve.count),), incs=((dve, 1),))
                co = 2 * hg
                w = [(dve, dve.count)]
                if hg >= 1:
                    w.append((act, ln_v[hg - 1]))
                sc.add("vector", lambda E: E.tensor_scalar(
                    tmp2[:], bnmv[:, :, 1], 1.0, EPS, ALU.mult, ALU.add),
                    waits=tuple(w), incs=((dve, 1),))
                tmp2_v = dve.count
                sc.add("vector", lambda E, co=co: E.tensor_copy(
                    meanall[:, co:co + 2], bnmv[:, :, 0]),
                    incs=((dve, 1),))
                sc.add("scalar", lambda E: E.activation(lnv[:], tmp2[:], AF.Ln),
                       waits=((dve, tmp2_v),), incs=((act, 1),))
                ln_v[hg] = act.count
                sc.add("scalar", lambda E, co=co: E.activation(
                    rstdall[:, co:co + 2], lnv[:], AF.Exp, scale=-0.5),
                    waits=((act, act.count),), incs=((act, 1),))
                rstd_v[hg] = act.count

            def emit_stats(hg):
                for j in range(4):
                    emit_stats_chunk(hg, j)
                emit_stats_tail(hg)

            def emit_applies(g):
                for t in range(4):
                    hg = 2 * g + t // 2
                    ti = t % 2
                    co = 4 * g + t
                    w = [(ld, ld_hg[hg]), (act, rstd_v[hg])]
                    if g >= 1:
                        w.append((pe, tp_v[(g - 1) * 8 + t * 2 + 1]))
                    sc.add("vector", lambda E, hg=hg, ti=ti, t=t, co=co: E.tensor_scalar(
                        xnb[:, t, :], xin[hg & 1][:, ti, :],
                        meanall[:, co:co + 1], rstdall[:, co:co + 1], ALU.subtract, ALU.mult),
                        waits=tuple(w), incs=((dve, 1),))
                    apply_v[(g, t)] = dve.count

            emit_xload(0)
            for dst, m in ((wkT, 1), (wvT, 2)):
                for hh in range(2):
                    sc.add("sync", lambda E, d=dst, m=m, hh=hh: E.dma_start(
                        out=d[:, hh * 4:(hh + 1) * 4, :],
                        in_=wscr[m, hh * 4:(hh + 1) * 4, :, :].rearrange("cc p d -> p cc d")),
                        incs=((ld, 16),))
            wk_ld = ld.count
            emit_stats(0)
            emit_stats(1)
            emit_applies(0)
            for g in range(NG):
                for t in range(4):
                    for hh in range(2):
                        idx = g * 8 + t * 2 + hh
                        r = idx & 3

                        def tmm(E, t=t, hh=hh, r=r):
                            inst = None
                            for i in range(4):
                                cc = hh * 4 + i
                                inst = E.matmul(ps_t[r][:, i * 128:(i + 1) * 128],
                                                xnb[:, t, cc * 128:(cc + 1) * 128],
                                                identb[:], start=True, stop=True)
                            return inst
                        w = [(dve, apply_v[(g, t)])]
                        if idx >= 4 and idx % 8 >= 2:
                            w.append(evac_v[idx - 4])
                        if idx % 8 < 2 and g >= 1:
                            w.append((act, vev_v[(g - 1) * 4 + 3]))
                        sc.add("tensor", tmm, waits=tuple(w), incs=((pe, 1),))
                        tp_v[idx] = pe.count

                        w = [(pe, tp_v[idx])]
                        if idx % 8 == 0 and g >= 1:
                            w.append((st, xnst_v[g - 1]))
                        if idx & 1:
                            sc.add("scalar", lambda E, t=t, hh=hh, r=r: E.copy(
                                xnT[:, hh * 4:(hh + 1) * 4, t * 128:(t + 1) * 128],
                                ps_t[r][:].rearrange("p (i c) -> p i c", i=4)),
                                waits=tuple(w), incs=((act, 1),))
                            evac_v[idx] = (act, act.count)
                        else:
                            sc.add("vector", lambda E, t=t, hh=hh, r=r: E.tensor_copy(
                                xnT[:, hh * 4:(hh + 1) * 4, t * 128:(t + 1) * 128],
                                ps_t[r][:].rearrange("p (i c) -> p i c", i=4)),
                                waits=tuple(w), incs=((dve, 1),))
                            evac_v[idx] = (dve, dve.count)

                def xnst(E, g=g):
                    E.dma_start(out=xnt_scr[0:4, :, g * G:(g + 1) * G].rearrange("cc p t -> p cc t"),
                                in_=xnT[:, 0:4, :]).then_inc(st.h, 16)
                    return E.dma_start(out=xnt_scr[4:8, :, g * G:(g + 1) * G].rearrange("cc p t -> p cc t"),
                                       in_=xnT[:, 4:8, :])
                sc.add("gpsimd", xnst, waits=(evac_v[g * 8 + 6], evac_v[g * 8 + 7]),
                       incs=((st, 16),))
                st.count += 16
                xnst_v[g] = st.count

                if g + 1 < NG:
                    emit_xload(g + 1)

                def emit_kn_dc(gk, dc):
                    w = [(act, ktev_v[(gk, dc)])]
                    if gk * 8 + dc >= 2:
                        pidx = gk * 8 + dc - 2
                        w.append((pe, knmm_v[(pidx // 8, pidx % 8)]))
                    sc.add("vector", lambda E, gk=gk, dc=dc: E.tensor_mul(
                        knsq[dc & 1][:], ktres[:, dc, gk * G:(gk + 1) * G],
                        ktres[:, dc, gk * G:(gk + 1) * G]),
                        waits=tuple(w), incs=((dve, 1),))
                    knsq_v[(gk, dc)] = dve.count
                    w = [(dve, knsq_v[(gk, dc)])]
                    if dc == 0 and gk >= 1:
                        w.append((dve, knrd_v[gk - 1]))
                    sc.add("tensor", lambda E, gk=gk, dc=dc: E.matmul(
                        ps_kn[0:1, :], onesb[:], knsq[dc & 1][:],
                        start=(dc == 0), stop=(dc == CCN - 1)),
                        waits=tuple(w), incs=((pe, 1),))
                    knmm_v[(gk, dc)] = pe.count

                def emit_kn_tail(gk):
                    def kn_slack(E):
                        E.matmul(ps_kn[32:33, 0:128], onescf[:], identf[:], start=True, stop=True)
                        return E.matmul(ps_kn[32:33, 0:128], onescf[:], identf[:], start=True, stop=True)
                    sc.add("tensor", kn_slack, incs=((pe, 1),))
                    kn_slack_v = pe.count
                    sc.add("vector", lambda E: E.reduce_max(hstat[0:1, 0:1], ps_kn[0:1, :], axis=AX.X),
                           waits=((pe, kn_slack_v),), incs=((dve, 1),))

                    def knmax(E, gk=gk):
                        if gk == 0:
                            return E.tensor_copy(ck2[:], hstat[0:1, 0:1])
                        return E.tensor_max(ck2[:], ck2[:], hstat[0:1, 0:1])
                    sc.add("vector", knmax, waits=((dve, dve.count),), incs=((dve, 1),))
                    knrd_v[gk] = dve.count

                for dc in range(CCN):
                    w = []
                    if dc == 0:
                        w.append(evac_v[g * 8 + 6])
                        w.append(evac_v[g * 8 + 7])
                        if g == 0:
                            w.append((ld, wk_ld))
                    if dc >= 2:
                        w.append((act, ktev_v[(g, dc - 2)]))

                    def ktmm(E, g=g, dc=dc):
                        inst = None
                        for cc in range(CCN):
                            inst = E.matmul(ps_kt[dc & 1][:],
                                            wkT[:, cc, dc * 128:(dc + 1) * 128],
                                            xnT[:, cc, :],
                                            start=(cc == 0), stop=(cc == CCN - 1))
                        return inst
                    sc.add("tensor", ktmm, waits=tuple(w), incs=((pe, 1),))
                    ktmm_v[(g, dc)] = pe.count
                    sc.add("scalar", lambda E, g=g, dc=dc: E.copy(
                        ktres[:, dc, g * G:(g + 1) * G], ps_kt[dc & 1][:]),
                        waits=((pe, ktmm_v[(g, dc)]),), incs=((act, 1),))
                    ktev_v[(g, dc)] = act.count
                    if g >= 1:
                        if dc < 4:
                            emit_kn_dc(g - 1, 2 * dc)
                            emit_kn_dc(g - 1, 2 * dc + 1)
                        elif dc == 4:
                            emit_kn_tail(g - 1)
                    if g + 1 < NG:
                        emit_stats_chunk(2 * g + 2 + dc // 4, dc % 4)
                        if dc == 3:
                            emit_stats_tail(2 * g + 2)
                        elif dc == 7:
                            emit_stats_tail(2 * g + 3)

                if g + 1 < NG:
                    emit_applies(g + 1)

                for tt in range(4):
                    gti = g * 4 + tt
                    pair = tt & 1
                    w = []
                    if tt == 0:
                        if g >= 1:
                            w.append((dve, knrd_v[g - 1]))
                        if g == 0:
                            w.append((ld, wk_ld))
                    if gti >= 2:
                        w.append((act, vev_v[gti - 2]))

                    def vmm(E, g=g, tt=tt, pair=pair):
                        pv = [ps_v, ps_kn] if pair == 0 else ps_t
                        inst = None
                        for cc in range(CCN):
                            for nh in range(2):
                                inst = E.matmul(pv[nh][:],
                                                xnT[:, cc, tt * 128:(tt + 1) * 128],
                                                wvT[:, cc, nh * 512:(nh + 1) * 512],
                                                start=(cc == 0), stop=(cc == CCN - 1))
                        return inst
                    sc.add("tensor", vmm, waits=tuple(w), incs=((pe, 1),))
                    vmm_v[gti] = pe.count

                    w = [(pe, vmm_v[gti])]
                    hh = tt // 2
                    if 2 * g + hh >= 2:
                        w.append((st, vst_v[2 * g + hh - 2]))

                    def vev(E, g=g, tt=tt, pair=pair, hh=hh):
                        pv = [ps_v, ps_kn] if pair == 0 else ps_t
                        E.copy(vout[(2 * g + hh) & 1][:, tt & 1, 0:512], pv[0][:])
                        return E.copy(vout[(2 * g + hh) & 1][:, tt & 1, 512:1024], pv[1][:])
                    sc.add("scalar", vev, waits=tuple(w), incs=((act, 1),))
                    vev_v[gti] = act.count

                    if tt & 1:
                        sc.add("gpsimd", lambda E, g=g, hh=hh: E.dma_start(
                            out=v_scr[g * G + hh * 256:g * G + (hh + 1) * 256, :].rearrange(
                                "(i p) d -> p i d", p=128),
                            in_=vout[(2 * g + hh) & 1][:]),
                            waits=((act, vev_v[gti]),), incs=((st, 16),))
                        vst_v[2 * g + hh] = st.count


            for dc in range(CCN):
                emit_kn_dc(NG - 1, dc)
            emit_kn_tail(NG - 1)

            add_barrier(sc, sm)
            sc.emit(nc, "a1")

        with ExitStack() as ctx:
            sb = lambda *a: ctx.enter_context(nc.sbuf_tensor(*a))
            ps = lambda *a: ctx.enter_context(nc.psum_tensor(*a))
            wqT = sb("wqT", [128, CCN, C], BF16)
            xnt_c = [sb(f"xnt_c{i}", [128, CCN, QC], BF16) for i in range(2)]
            qt = [sb(f"qt{i}", [128, CCN, QC], BF16) for i in range(2)]
            vstream = sb("vstream", [128, 8, C], BF16)
            pT = sb("pT", [128, 2, QC], BF16)
            ctxout = [sb(f"ctxout{i}", [128, CCN, QC], BF16) for i in range(2)]
            qsq = [sb(f"qsq{i}", [128, QC], BF16) for i in range(2)]
            dacc = sb("dacc", [128, QC], F32)
            ps_ctx = [ps(f"ps_ctx{i}", [128, 512], F32) for i in range(4)]
            ps_st = [ps(f"ps_st{i}", [128, 512], F32) for i in range(2)]
            ps_misc = ps("ps_misc", [128, 512], F32)
            ps_den = ps("ps_den", [128, 512], F32)

            sc = Sched()
            for hh in range(2):
                sc.add("sync", lambda E, hh=hh: E.dma_start(
                    out=wqT[:, hh * 4:(hh + 1) * 4, :],
                    in_=wscr[0, hh * 4:(hh + 1) * 4, :, :].rearrange("cc p d -> p cc d")),
                    incs=((ld, 16),))
            wq_ld = ld.count
            xntc_v = {}
            dadd_v = {}; denmm_v = {}
            qpmm_v = {}; qev_v = {}; qsq_v = {}; qnmm_v = {}
            negneg_v = {}; negcp_v = {}
            stmm_v = {}; exp_v = {}; pvmm_v = {}
            vld_v = {}
            ctxev_v = {}; denev_v = {}; dcol_v = {}; dinv_v = {}; ctxst_v = {}

            for j in range(NQ):
                w = []
                if j >= 2:
                    w.append((pe, qpmm_v[(j - 2, CCN - 1)]))

                def xcl(E, j=j):
                    E.dma_start(out=xnt_c[j & 1][:, 0:4, :],
                                in_=xnt_scr[0:4, :, j * QC:(j + 1) * QC].rearrange("cc p t -> p cc t")
                                ).then_inc(ld.h, 16)
                    return E.dma_start(out=xnt_c[j & 1][:, 4:8, :],
                                       in_=xnt_scr[4:8, :, j * QC:(j + 1) * QC].rearrange("cc p t -> p cc t"))
                sc.add("sync", xcl, waits=tuple(w), incs=((ld, 16),))
                ld.count += 16
                xntc_v[j] = ld.count

                if j >= 1:
                    emit_tail_act(j - 1)

                for dc in range(CCN):
                    w = []
                    if dc == 0:
                        w.append((ld, xntc_v[j]))
                        if j == 0:
                            w.append((ld, wq_ld))
                        if j >= 1:
                            w.append((act, exp_v[(j - 1, NKT - 1)]))
                    if dc >= 2:
                        w.append((act, qev_v[(j, dc - 2)]))

                    def qpm(E, j=j, dc=dc):
                        inst = None
                        for cc in range(CCN):
                            inst = E.matmul(ps_st[dc & 1][:, 0:QC],
                                            wqT[:, cc, dc * 128:(dc + 1) * 128],
                                            xnt_c[j & 1][:, cc, :],
                                            start=(cc == 0), stop=(cc == CCN - 1))
                        return inst
                    sc.add("tensor", qpm, waits=tuple(w), incs=((pe, 1),))
                    qpmm_v[(j, dc)] = pe.count

                    w = [(pe, qpmm_v[(j, dc)])]
                    if j >= 2 and dc == 0:
                        w.append((pe, stmm_v[(j - 2, NKT - 1)]))
                    sc.add("scalar", lambda E, j=j, dc=dc: E.activation(
                        qt[j & 1][:, dc, :], ps_st[dc & 1][:, 0:QC], AF.Copy, scale=1.0 / 32.0),
                        waits=tuple(w), incs=((act, 1),))
                    qev_v[(j, dc)] = act.count
                    if dc == 1 and j >= 1:
                        emit_denev(j - 1, qpmm_v[(j, 1)])

                for dc in range(CCN):
                    w = [(act, qev_v[(j, dc)])]
                    if j * 8 + dc >= 2:
                        pi = j * 8 + dc - 2
                        w.append((pe, qnmm_v[(pi // 8, pi % 8)]))
                    sc.add("vector", lambda E, j=j, dc=dc: E.tensor_mul(
                        qsq[dc & 1][:], qt[j & 1][:, dc, :], qt[j & 1][:, dc, :]),
                        waits=tuple(w), incs=((dve, 1),))
                    qsq_v[(j, dc)] = dve.count
                    w = [(dve, qsq_v[(j, dc)])]
                    if dc == 0 and j >= 1:
                        w.append((dve, negneg_v[j - 1]))
                    sc.add("tensor", lambda E, j=j, dc=dc: E.matmul(
                        ps_misc[0:1, QC:2 * QC], onesb[:], qsq[dc & 1][:],
                        start=(dc == 0), stop=(dc == CCN - 1)),
                        waits=tuple(w), incs=((pe, 1),))
                    qnmm_v[(j, dc)] = pe.count

                def emit_vload(kt):
                    q4 = kt // 4
                    w = []
                    if q4 >= 2:
                        w.append((pe, pvmm_v[(j, 4 * q4 - 5)]))
                    elif j >= 1:
                        w.append((pe, pvmm_v[(j - 1, 59 if q4 == 0 else 63)]))
                    slot = (q4 * 4) % 8
                    sc.add("sync", lambda E, j=j, kt=kt, slot=slot: E.dma_start(
                        out=vstream[:, slot:slot + 4, :],
                        in_=v_scr[kt * 128:kt * 128 + 512, :].rearrange("(i p) d -> p i d", p=128)),
                        waits=tuple(w), incs=((ld, 16),))
                    vld_v[(j, q4)] = ld.count

                def emit_st(kt):
                    w = []
                    if kt >= 2:
                        w.append((act, exp_v[(j, kt - 2)]))
                    else:
                        w.append((act, qev_v[(j, CCN - 1)]))
                    sc.add("tensor", lambda E, j=j, kt=kt: [
                        E.matmul(ps_st[kt & 1][:, 0:QC],
                                 ktres[:, dc, kt * 128:(kt + 1) * 128],
                                 qt[j & 1][:, dc, :],
                                 start=(dc == 0), stop=(dc == CCN - 1))
                        for dc in range(CCN)][-1],
                        waits=tuple(w), incs=((pe, 1),))
                    stmm_v[(j, kt)] = pe.count

                def emit_exp(kt):
                    w = [(pe, stmm_v[(j, kt)])]
                    if kt == 0:
                        w.append((act, negcp_v[j]))
                    if kt >= 2:
                        w.append((pe, pvmm_v[(j, kt - 2)]))
                        w.append((dve, dadd_v[(j, kt - 2)]))
                    elif j >= 1:
                        w.append((dve, dadd_v[(j - 1, NKT - 2 + kt)]))
                    sc.add("scalar", lambda E, j=j, kt=kt: E.activation(
                        pT[:, kt & 1, :], ps_st[kt & 1][:, 0:QC], AF.Exp, bias=negc[:]),
                        waits=tuple(w), incs=((act, 1),))
                    exp_v[(j, kt)] = act.count

                def emit_pv(kt):
                    w = [(act, exp_v[(j, kt)]), (ld, vld_v[(j, kt // 4)])]
                    if kt == 0:
                        if j >= 2:
                            w.append((st, ctxst_v[j - 2]))
                        if j >= 1:
                            w.append((dve, ctxev_v[j - 1]))
                            w.append((dve, dinv_v[j - 1]))

                    def pvm(E, j=j, kt=kt):
                        inst = None
                        for dc in range(CCN):
                            inst = E.matmul(ps_ctx[dc >> 1][:, (dc & 1) * QC:((dc & 1) + 1) * QC],
                                            vstream[:, kt % 8, dc * 128:(dc + 1) * 128],
                                            pT[:, kt & 1, :],
                                            start=(kt == 0 and (dc & 1) == 0),
                                            stop=(kt == NKT - 1))
                        return inst
                    sc.add("tensor", pvm, waits=tuple(w), incs=((pe, 1),))
                    pvmm_v[(j, kt)] = pe.count
                    if kt == 0:
                        wd = [(act, exp_v[(j, 0)])]
                        if j >= 1:
                            wd.append((pe, denmm_v[j - 1]))
                        sc.add("vector", lambda E, j=j: E.tensor_copy(dacc[:], pT[:, 0, :]),
                               waits=tuple(wd), incs=((dve, 1),))
                    else:
                        sc.add("vector", lambda E, j=j, kt=kt: E.tensor_add(
                            dacc[:], dacc[:], pT[:, kt & 1, :]),
                            waits=((act, exp_v[(j, kt)]),), incs=((dve, 1),))
                    dadd_v[(j, kt)] = dve.count

                def emit_tail_act(jj):
                    w = [(dve, dadd_v[(jj, NKT - 1)])]
                    if jj >= 1:
                        w.append((act, denev_v[jj - 1]))
                    sc.add("tensor", lambda E, j=jj: E.matmul(
                        ps_den[0:1, 0:QC], onescf[:], dacc[:], start=True, stop=True),
                        waits=tuple(w), incs=((pe, 1),))
                    denmm_v[jj] = pe.count

                    w = [(pe, pvmm_v[(jj, NKT - 1)]), (pe, denmm_v[jj])]
                    if j >= 2:
                        w.append((st, ctxst_v[j - 2]))

                    def ctxevd(E, j=jj):
                        inst = None
                        for c4 in range(4):
                            inst = E.tensor_copy(ctxout[j & 1][:, 2 * c4:2 * c4 + 2, :],
                                                 ps_ctx[c4][:].rearrange("p (i c) -> p i c", i=2))
                        return inst
                    sc.add("vector", ctxevd, waits=tuple(w), incs=((dve, 1),))
                    ctxev_v[jj] = dve.count

                def emit_denev(jj, slack_pe):
                    def denev(E):
                        E.copy(den_sb[:], ps_den[0:1, 0:QC])
                        E.copy(dmyr[:], den_sb[:])
                        E.copy(dmy[0:1, 4:5], dmyr[0:1, 0:1])
                        return E.copy(dmyr[:], den_sb[:])
                    sc.add("scalar", denev, waits=((pe, slack_pe),), incs=((act, 1),))
                    denev_v[jj] = act.count

                def emit_tail_pe(jj):
                    def dcol(E):
                        E.matmul(ps_den[:, QC:QC + 1], den_sb[0:1, 0:128], ones1f[:],
                                 start=True, stop=True)
                        E.matmul(ps_den[:, QC + 1:QC + 2], den_sb[0:1, 128:256], ones1f[:],
                                 start=True, stop=True)
                        E.matmul(ps_den[:, QC + 10:QC + 138], onesrf[:], onesrf[:],
                                 start=True, stop=True)
                        return E.matmul(ps_den[:, QC + 10:QC + 138], onesrf[:], onesrf[:],
                                            start=True, stop=True)
                    sc.add("tensor", dcol, waits=((act, denev_v[jj]),), incs=((pe, 1),))
                    dcol_v[jj] = pe.count

                    sc.add("vector", lambda E, j=jj: E.reciprocal(
                        dinv_all[:, 2 * j:2 * j + 2], ps_den[:, QC:QC + 2]),
                        waits=((pe, dcol_v[jj]),), incs=((dve, 1),))
                    dinv_v[jj] = dve.count

                    def cst(E, j=jj):
                        E.dma_start(out=ctx_scr[0:4, :, j * QC:(j + 1) * QC].rearrange("cc p t -> p cc t"),
                                        in_=ctxout[j & 1][:, 0:4, :]).then_inc(st.h, 16)
                        return E.dma_start(out=ctx_scr[4:8, :, j * QC:(j + 1) * QC].rearrange("cc p t -> p cc t"),
                                               in_=ctxout[j & 1][:, 4:8, :])
                    sc.add("gpsimd", cst, waits=((dve, ctxev_v[jj]),), incs=((st, 16),))
                    st.count += 16
                    ctxst_v[jj] = st.count

                if j >= 1:
                    emit_tail_pe(j - 1)

                def qn_slack(E):
                    E.matmul(ps_st[0][:, 0:128], onesrf[:], onesrf[:], start=True, stop=True)
                    return E.matmul(ps_st[0][:, 0:128], onesrf[:], onesrf[:], start=True, stop=True)
                sc.add("tensor", qn_slack, incs=((pe, 1),))
                qnsl_v = pe.count

                emit_vload(0)
                emit_st(0)
                emit_st(1)

                sc.add("vector", lambda E: E.reduce_max(cs[0:1, 0:1], ps_misc[0:1, QC:2 * QC], axis=AX.X),
                       waits=((pe, qnsl_v),), incs=((dve, 1),))
                sc.add("vector", lambda E: E.tensor_mul(cs[0:1, 1:2], cs[0:1, 0:1], ck2[:]),
                       waits=((dve, dve.count),), incs=((dve, 1),))
                c2_v = dve.count
                if _DBG:
                    def _dn0(E, j=j):
                        E.tensor_copy(negs_sb[0:1, j:j + 1], cs[0:1, 0:1])
                        return E.tensor_copy(negs_sb[0:1, 16 + j:17 + j], cs[0:1, 1:2])
                    sc.add("vector", _dn0, incs=((dve, 1),))

                sc.add("scalar", lambda E: E.activation(cs[0:1, 2:3], cs[0:1, 1:2], AF.Ln),
                       waits=((dve, c2_v),), incs=((act, 1),))
                sc.add("scalar", lambda E: E.activation(cs[0:1, 3:4], cs[0:1, 2:3], AF.Exp, scale=0.5),
                       waits=((act, act.count),), incs=((act, 1),))
                cexp_v = act.count
                if _DBG:
                    sc.add("scalar", lambda E, j=j: E.copy(negs_sb[0:1, 32 + j:33 + j], cs[0:1, 3:4]),
                           incs=((act, 1),))

                sc.add("vector", lambda E: E.tensor_scalar(cs[0:1, 4:5], cs[0:1, 3:4], -1.02, None, ALU.mult),
                       waits=((act, cexp_v),), incs=((dve, 1),))
                negneg_v[j] = dve.count

                def bc(E):
                    E.matmul(ps_misc[:, 300:301], onesrf[:], cs[0:1, 4:5], start=True, stop=True)
                    E.matmul(ps_misc[:, 310:438], onesrf[:], onesrf[:], start=True, stop=True)
                    return E.matmul(ps_misc[:, 310:438], onesrf[:], onesrf[:], start=True, stop=True)
                sc.add("tensor", bc, waits=((dve, negneg_v[j]),), incs=((pe, 1),))
                bc_v = pe.count
                sc.add("scalar", lambda E: E.copy(negc[:], ps_misc[:, 300:301]),
                       waits=((pe, bc_v),), incs=((act, 1),))
                negcp_v[j] = act.count
                if _DBG:
                    sc.add("vector", lambda E, j=j: E.tensor_copy(
                        negs_sb[0:1, 48 + j:49 + j], cs[0:1, 4:5]),
                        waits=((dve, negneg_v[j]),), incs=((dve, 1),))

                emit_exp(0)
                for kt in range(2, NKT):
                    if kt % 4 == 0:
                        emit_vload(kt)
                    emit_st(kt)
                    emit_exp(kt - 1)
                    emit_pv(kt - 2)
                emit_exp(NKT - 1)
                emit_pv(NKT - 2)
                emit_pv(NKT - 1)

                if j == NQ - 1:
                    emit_tail_act(j)
                    def last_slack(E):
                        E.matmul(ps_misc[:, 310:438], onesrf[:], onesrf[:], start=True, stop=True)
                        return E.matmul(ps_misc[:, 310:438], onesrf[:], onesrf[:], start=True, stop=True)
                    sc.add("tensor", last_slack, incs=((pe, 1),))
                    emit_denev(j, pe.count)
                    emit_tail_pe(j)

            add_barrier(sc, sm)
            sc.emit(nc, "attn")

        mctx.close()

        with ExitStack() as ctx:
            sb = lambda *a: ctx.enter_context(nc.sbuf_tensor(*a))
            ps = lambda *a: ctx.enter_context(nc.psum_tensor(*a))
            woT = sb("woT", [128, CCN, C], BF16)
            wrT = sb("wrT", [128, CCN, C], BF16)
            wkfT = sb("wkfT", [128, CCN, C], BF16)
            wvfT = sb("wvfT", [128, CCN, C], BF16)
            xnt_d = [sb(f"xnt_d{i}", [128, CCN, G + 1], BF16) for i in range(2)]
            xin_d = [sb(f"xin_d{i}", [128, 4, C], F32) for i in range(2)]
            ctx_d = [sb(f"ctx_d{i}", [128, CCN, G], BF16) for i in range(2)]
            xr = sb("xr", [128, CCN, G], BF16)
            xk = sb("xk", [128, CCN, G], BF16)
            dtmp = sb("dtmp", [128, G], BF16)
            kact = sb("kact", [128, CCN, G], BF16)
            krelu = [sb(f"krelu{i}", [128, G], F32) for i in range(2)]
            rnat = sb("rnat", [128, 4, C], BF16)
            t1 = sb("t1", [128, C], F32)
            t2 = sb("t2", [128, C], F32)
            outst = sb("outst", [128, 4, C], F32)
            ps_kf = [ps(f"ps_kf{i}", [128, 512], F32) for i in range(2)]
            ps_r = ps("ps_r", [128, 1024], F32)
            ps_vf = ps("ps_vf", [128, 1024], F32)
            ps_ao = ps("ps_ao", [128, 1024], F32)

            sc = Sched()
            wdld_v = {}
            for dst, m in ((wkfT, 5), (wrT, 4), (wvfT, 6), (woT, 3)):
                for hh in range(2):
                    sc.add("gpsimd", lambda E, d=dst, m=m, hh=hh: E.dma_start(
                        out=d[:, hh * 4:(hh + 1) * 4, :],
                        in_=wscr[m, hh * 4:(hh + 1) * 4, :, :].rearrange("cc p d -> p cc d")),
                        incs=((st, 16),))
                wdld_v[m] = st.count

            xntd_v = {}; xind_v = {}; ctxd_v = {}
            mix_v = {}; mixk_v = {}
            kfmm_v = {}; relu_v = {}; sqd_v = {}
            rmm_v = {}; sig_v = {}; vfmm_v = {}; aomm_v = {}
            comb_v = {}; outst_v = {}

            def emit_loads(d):
                w = []
                if d >= 2:
                    w.append((dve, mix_v[d - 2]))

                def xntl(E, d=d):
                    if d == 0:
                        E.dma_start(out=xnt_d[0][:, 0:4, 1:G + 1],
                                    in_=xnt_scr[0:4, :, 0:G].rearrange("cc p t -> p cc t")
                                    ).then_inc(ld.h, 16)
                        return E.dma_start(out=xnt_d[0][:, 4:8, 1:G + 1],
                                           in_=xnt_scr[4:8, :, 0:G].rearrange("cc p t -> p cc t"))
                    lo = d * G - 1
                    E.dma_start(out=xnt_d[d & 1][:, 0:4, :],
                                in_=xnt_scr[0:4, :, lo:lo + G + 1].rearrange("cc p t -> p cc t")
                                ).then_inc(ld.h, 16)
                    return E.dma_start(out=xnt_d[d & 1][:, 4:8, :],
                                       in_=xnt_scr[4:8, :, lo:lo + G + 1].rearrange("cc p t -> p cc t"))
                sc.add("sync", xntl, waits=tuple(w), incs=((ld, 16),))
                ld.count += 16
                xntd_v[d] = ld.count

                w = []
                if d >= 2:
                    w.append((dve, comb_v[(d - 2, 3)]))
                sc.add("sync", lambda E, d=d: E.dma_start(
                    out=xin_d[d & 1][:],
                    in_=x_d[d * G:(d + 1) * G, :].rearrange("(i p) c -> p i c", p=128)),
                    waits=tuple(w), incs=((ld, 16),))
                xind_v[d] = ld.count

                w = []
                if d >= 2:
                    w.append((pe, aomm_v[(d - 2, 3)]))

                def ctxl(E, d=d):
                    E.dma_start(out=ctx_d[d & 1][:, 0:4, :],
                                in_=ctx_scr[0:4, :, d * G:(d + 1) * G].rearrange("cc p t -> p cc t")
                                ).then_inc(ld.h, 16)
                    return E.dma_start(out=ctx_d[d & 1][:, 4:8, :],
                                       in_=ctx_scr[4:8, :, d * G:(d + 1) * G].rearrange("cc p t -> p cc t"))
                sc.add("sync", ctxl, waits=tuple(w), incs=((ld, 16),))
                ld.count += 16
                ctxd_v[d] = ld.count

            def emit_mixk(d):
                def mixes_k(E, d=d):
                    inst = None
                    for cc in range(CCN):
                        E.tensor_sub(dtmp[:], xnt_d[d & 1][:, cc, 1:G + 1], xnt_d[d & 1][:, cc, 0:G])
                        inst = E.scalar_tensor_tensor(xk[:, cc, :], dtmp[:], muk_sb[:, cc:cc + 1],
                                                      xnt_d[d & 1][:, cc, 0:G], ALU.mult, ALU.add)
                    return inst
                w = [(ld, xntd_v[d])]
                if d == 0:
                    w.append((dve, dve.count))
                if d >= 1:
                    w.append((pe, kfmm_v[(d - 1, CCN - 1)]))
                sc.add("vector", mixes_k, waits=tuple(w), incs=((dve, 1),))
                mixk_v[d] = dve.count

            def emit_mixr(d):
                def mixes_r(E, d=d):
                    inst = None
                    for cc in range(CCN):
                        E.tensor_sub(dtmp[:], xnt_d[d & 1][:, cc, 1:G + 1], xnt_d[d & 1][:, cc, 0:G])
                        inst = E.scalar_tensor_tensor(xr[:, cc, :], dtmp[:], mur_sb[:, cc:cc + 1],
                                                      xnt_d[d & 1][:, cc, 0:G], ALU.mult, ALU.add)
                    return inst
                w = [(ld, xntd_v[d])]
                if d >= 1:
                    w.append((pe, rmm_v[(d - 1, 3)]))
                sc.add("vector", mixes_r, waits=tuple(w), incs=((dve, 1),))
                mix_v[d] = dve.count

            emit_loads(0)
            sc.add("vector", lambda E: E.tensor_copy(xnt_d[0][:, :, 0:1], xnhT[:].unsqueeze(2)),
                   waits=((ld, xntd_v[0]),), incs=((dve, 1),))
            emit_mixk(0)
            emit_mixr(0)

            for d in range(ND):
                for dc in range(CCN):
                    def kfm(E, d=d, dc=dc):
                        inst = None
                        for cc in range(CCN):
                            inst = E.matmul(ps_kf[dc & 1][:],
                                            wkfT[:, cc, dc * 128:(dc + 1) * 128],
                                            xk[:, cc, :],
                                            start=(cc == 0), stop=(cc == CCN - 1))
                        return inst
                    w = []
                    if dc >= 2:
                        w.append((act, relu_v[(d, dc - 2)]))
                    if dc == 0:
                        w.append((dve, mixk_v[d]))
                        if d == 0:
                            w.append((st, wdld_v[5]))
                    sc.add("tensor", kfm, waits=tuple(w), incs=((pe, 1),))
                    kfmm_v[(d, dc)] = pe.count

                    w = [(pe, kfmm_v[(d, dc)])]
                    if d * 8 + dc >= 2:
                        pi = d * 8 + dc - 2
                        w.append((dve, sqd_v[(pi // 8, pi % 8)]))
                    sc.add("scalar", lambda E, d=d, dc=dc: E.activation(
                        krelu[dc & 1][:], ps_kf[dc & 1][:], AF.Relu),
                        waits=tuple(w), incs=((act, 1),))
                    relu_v[(d, dc)] = act.count

                    w = [(act, relu_v[(d, dc)])]
                    if d >= 1 and dc == 0:
                        w.append((pe, vfmm_v[(d - 1, 3)]))
                    sc.add("vector", lambda E, d=d, dc=dc: E.tensor_mul(
                        kact[:, dc, :], krelu[dc & 1][:], krelu[dc & 1][:]),
                        waits=tuple(w), incs=((dve, 1),))
                    sqd_v[(d, dc)] = dve.count

                if d + 1 < ND:
                    emit_loads(d + 1)
                    emit_mixk(d + 1)

                for tt in range(4):
                    w = []
                    if tt == 0:
                        w.append((dve, mix_v[d]))
                    prev = (d, tt - 1) if tt >= 1 else (d - 1, 3)
                    if prev[0] >= 0:
                        w.append((act, sig_v[prev]))
                        if tt == 0:
                            w.append((st, wdld_v[4]))

                    def rmm(E, d=d, tt=tt):
                        inst = None
                        for cc in range(CCN):
                            for nh in range(2):
                                inst = E.matmul(ps_r[:, nh * 512:(nh + 1) * 512],
                                                xr[:, cc, tt * 128:(tt + 1) * 128],
                                                wrT[:, cc, nh * 512:(nh + 1) * 512],
                                                start=(cc == 0), stop=(cc == CCN - 1))
                        return inst
                    if d == 0 and tt == 0:
                        w.append((st, wdld_v[4]))
                    sc.add("tensor", rmm, waits=tuple(w), incs=((pe, 1),))
                    rmm_v[(d, tt)] = pe.count
                    if tt == 3 and d + 1 < ND:
                        emit_mixr(d + 1)

                    w = [(pe, rmm_v[(d, tt)])]
                    if d >= 1 and tt == 0:
                        w.append((dve, comb_v[(d - 1, 3)]))
                    sc.add("scalar", lambda E, d=d, tt=tt: E.activation(
                        rnat[:, tt, :], ps_r[:], AF.Sigmoid),
                        waits=tuple(w), incs=((act, 1),))
                    sig_v[(d, tt)] = act.count

                    w = [(dve, sqd_v[(d, CCN - 1)])]
                    prev = (d, tt - 1) if tt >= 1 else (d - 1, 3)
                    if prev[0] >= 0:
                        w.append((dve, comb_v[prev]))
                    if d == 0 and tt == 0:
                        w.append((st, wdld_v[6]))

                    def vfm(E, d=d, tt=tt):
                        inst = None
                        for dc in range(CCN):
                            for nh in range(2):
                                inst = E.matmul(ps_vf[:, nh * 512:(nh + 1) * 512],
                                                kact[:, dc, tt * 128:(tt + 1) * 128],
                                                wvfT[:, dc, nh * 512:(nh + 1) * 512],
                                                start=(dc == 0), stop=(dc == CCN - 1))
                        return inst
                    sc.add("tensor", vfm, waits=tuple(w), incs=((pe, 1),))
                    vfmm_v[(d, tt)] = pe.count

                    w = [(ld, ctxd_v[d])]
                    if prev[0] >= 0:
                        w.append((dve, comb_v[prev]))
                    if d == 0 and tt == 0:
                        w.append((st, wdld_v[3]))

                    def aom(E, d=d, tt=tt):
                        inst = None
                        for dc in range(CCN):
                            for nh in range(2):
                                inst = E.matmul(ps_ao[:, nh * 512:(nh + 1) * 512],
                                                ctx_d[d & 1][:, dc, tt * 128:(tt + 1) * 128],
                                                woT[:, dc, nh * 512:(nh + 1) * 512],
                                                start=(dc == 0), stop=(dc == CCN - 1))
                        return inst
                    sc.add("tensor", aom, waits=tuple(w), incs=((pe, 1),))
                    aomm_v[(d, tt)] = pe.count

                    w = [(pe, aomm_v[(d, tt)]), (pe, vfmm_v[(d, tt)]),
                         (act, sig_v[(d, tt)]), (ld, xind_v[d])]
                    if d >= 1 and tt == 0:
                        w.append((st, outst_v[d - 1]))

                    def comb(E, d=d, tt=tt):
                        E.scalar_tensor_tensor(t1[:], ps_ao[:], dinv_all[:, 4 * d + tt:4 * d + tt + 1],
                                               xin_d[d & 1][:, tt, :], ALU.mult, ALU.add)
                        E.tensor_mul(t2[:], ps_vf[:], rnat[:, tt, :])
                        return E.tensor_add(outst[:, tt, :], t1[:], t2[:])
                    sc.add("vector", comb, waits=tuple(w), incs=((dve, 1),))
                    comb_v[(d, tt)] = dve.count

                sc.add("gpsimd", lambda E, d=d: E.dma_start(
                    out=y_d[d * G:(d + 1) * G, :].rearrange("(i p) c -> p i c", p=128),
                    in_=outst[:]),
                    waits=((dve, comb_v[(d, 3)]),), incs=((st, 16),))
                outst_v[d] = st.count

            if _DBG:
                for dst, src in ((dbg_dinv, dinv_all), (dbg_negc, negc),
                                 (dbg_cs, cs), (dbg_ck2, ck2), (dbg_xnhT, xnhT),
                                 (dbg_negs, negs_sb), (dbg_rstd, rstdall),
                                 (dbg_mean, meanall)):
                    sc.add("sync", lambda E, d=dst, s=src: E.dma_start(out=d, in_=s[:]),
                           incs=((st, 16),))

            sc.add("gpsimd", None, waits=((st, st.count),))
            sc.emit(nc, "final")

    return nc


_NC_CACHE = None


def kernel(L0_attn_x, L0_attn_W_qkv, L0_attn_W_out, L0_ffn_W_r, L0_ffn_W_k,
           L0_ffn_W_v, L0_ffn_mu_r, L0_ffn_mu_k):
    global _NC_CACHE
    x = np.asarray(L0_attn_x, dtype=np.float32)
    wqkv = np.ascontiguousarray(np.asarray(L0_attn_W_qkv, dtype=np.float32))
    wout = np.ascontiguousarray(np.asarray(L0_attn_W_out, dtype=np.float32))
    wr = np.ascontiguousarray(np.asarray(L0_ffn_W_r, dtype=np.float32))
    wkf = np.ascontiguousarray(np.asarray(L0_ffn_W_k, dtype=np.float32))
    wvf = np.ascontiguousarray(np.asarray(L0_ffn_W_v, dtype=np.float32))
    mus = np.ascontiguousarray(
        np.concatenate([np.asarray(L0_ffn_mu_r, dtype=np.float32).reshape(-1),
                        np.asarray(L0_ffn_mu_k, dtype=np.float32).reshape(-1)]).reshape(1, 2 * C))

    consts = {
        "identb": np.eye(128, dtype=ml_dtypes.bfloat16),
        "identf": np.eye(128, dtype=np.float32),
        "onesb": np.ones((128, 1), dtype=ml_dtypes.bfloat16),
        "ones1b": np.ones((1, 1), dtype=ml_dtypes.bfloat16),
        "onesrf": np.ones((1, 128), dtype=np.float32),
        "onescf": np.ones((128, 1), dtype=np.float32),
        "ones1f": np.ones((1, 1), dtype=np.float32),
    }

    B = x.shape[0]
    in_maps = []
    for core in range(8):
        b, h = core // 2, core % 2
        own = x[b, h * OWN:(h + 1) * OWN]
        oth = x[b, (1 - h) * OWN:(2 - h) * OWN]
        xp = np.ascontiguousarray(np.concatenate([own, oth], axis=0))
        xh = (np.zeros((1, C), np.float32) if h == 0
              else np.ascontiguousarray(x[b, OWN - 1:OWN]))
        in_maps.append({
            "x": xp, "xh": xh, "wqkv": wqkv, "wout": wout,
            "wr": wr, "wkf": wkf, "wvf": wvf, "mus": mus, **consts,
        })

    if _NC_CACHE is None:
        _NC_CACHE = build_program()
    res = run_bass_kernel_spmd(_NC_CACHE, in_maps, core_ids=list(range(8)))

    out = np.empty((B, S, C), dtype=np.float32)
    for core in range(8):
        b, h = core // 2, core % 2
        out[b, h * OWN:(h + 1) * OWN] = res.results[core]["y"]
    return out


if __name__ == "__main__":
    nc = build_program()
    print("built ok; instructions:", nc.n_instructions())
